# version 06
# baseline (speedup 1.0000x reference)
import numpy as np
import concourse.bass as bass
import concourse.mybir as mybir
from concourse.bass_utils import run_bass_kernel_spmd
from concourse.alu_op_type import AluOpType

F32 = mybir.dt.float32
BF16 = mybir.dt.bfloat16
U8 = mybir.dt.uint8
AF = mybir.ActivationFunctionType
AX = mybir.AxisListType.X

B, T, C = 2, 8192, 1024
NH, NKV, HD = 16, 4, 64
E, FF, GD = 8, 4096, 16
TQ = 2048
NT512 = 16
LN_EPS = 1e-5
RMS_EPS = 1e-6
DBG = False
DBG_RES = None


class Plan:

    def __init__(self, nc):
        self.nc = nc
        self.ops = {k: [] for k in ("sync", "gpsimd", "tensor", "vector", "scalar")}
        self.sem = {}
        self.count = {}

    def add_sem(self, name):
        self.sem[name] = self.nc.semaphore(name).__enter__()
        self.count[name] = 0

    def op(self, eng, fn, waits=(), sig=None, inc=0):
        ws = [(s, v) for (s, v) in waits if v > 0]
        if sig is not None:
            self.count[sig] += inc
        self.ops[eng].append((fn, ws, sig, inc))
        return self.count[sig] if sig is not None else None

    def cur(self, sem):
        return self.count[sem]

    def emit(self, eng_name, eng):
        for fn, waits, sig, inc in self.ops[eng_name]:
            best = {}
            for (s, v) in waits:
                best[s] = max(best.get(s, 0), v)
            for s, v in best.items():
                eng.wait_ge(self.sem[s], v)
            ins = fn(eng)
            if sig is not None and ins is not None:
                ins.then_inc(self.sem[sig], inc)


def build_program():
    nc = bass.Bass("TRN2", target_bir_lowering=False, debug=False)

    def din(name, shape):
        return nc.dram_tensor(name, shape, F32, kind="ExternalInput").ap()

    xT = din("xT", [C, T])
    xtm = din("xtm", [T, C])
    xoT = din("xoT", [C, TQ])
    xo = din("xo", [TQ, C])
    wq = din("wq", [C, 256])
    wkd = din("wkd", [C, 128])
    wv = din("wv", [C, 64])
    wproj = din("wproj", [256, C])
    wred = din("wred", [C, GD])
    wg = din("wg", [E, GD])
    wgT = din("wgT", [GD, E])
    w1 = din("w1", [E, C, FF])
    b1r = din("b1r", [128, E * 32])
    w2 = din("w2", [E, FF, C])
    b2c = din("b2c", [1, E * C])
    cosf = din("cosf", [128, T])
    sinf = din("sinf", [128, T])
    rotm = din("rotm", [128, 128])
    sel128 = din("sel128", [128, 128])
    sel2 = din("sel2", [2, 128])
    hmask2 = din("hmask2", [128, 2])
    identf = din("identf", [128, 128])
    maskP = din("maskP", [128, 896])
    rowsel = din("rowsel", [2, 256])
    onesc = din("onesc", [128, 2])
    onesr = din("onesr", [1, 128])

    pp_dram = nc.dram_tensor("pp_dram", [T, C], F32, kind="Internal").ap()
    attn_own = nc.dram_tensor("attn_own", [TQ, C], F32, kind="Internal").ap()
    vecb = nc.dram_tensor("vecb", [E, 1], F32, kind="Internal").ap()
    out = nc.dram_tensor("out", [TQ, C], F32, kind="ExternalOutput").ap()
    if DBG:
        d_xn = nc.dram_tensor("d_xn", [128, 8 * TQ], F32, kind="ExternalOutput").ap()
        d_q = nc.dram_tensor("d_q", [128, 16384], F32, kind="ExternalOutput").ap()
        d_k = nc.dram_tensor("d_k", [128, 8192], F32, kind="ExternalOutput").ap()
        d_v = nc.dram_tensor("d_v", [128, 4096], F32, kind="ExternalOutput").ap()
        d_msc = nc.dram_tensor("d_msc", [128, 128], F32, kind="ExternalOutput").ap()
        d_pp = nc.dram_tensor("d_pp", [T, C], F32, kind="ExternalOutput").ap()
        d_stat = nc.dram_tensor("d_stat", [2, NT512 * 512], F32, kind="ExternalOutput").ap()
        d_lg = nc.dram_tensor("d_lg", [128, 16 * E], F32, kind="ExternalOutput").ap()
        dd_wgn = nc.dram_tensor("d_wgn", [GD, E], F32, kind="ExternalOutput").ap()
        d_red = nc.dram_tensor("d_red", [GD, 512], F32, kind="ExternalOutput").ap()
        d_yhp = nc.dram_tensor("d_yhp", [32 * 128, 512], F32, kind="ExternalOutput").ap()
        d_acc0 = nc.dram_tensor("d_acc0", [128, 16384], F32, kind="ExternalOutput").ap()
        d_h0 = nc.dram_tensor("d_h0", [128, 2048], F32, kind="ExternalOutput").ap()
        d_xnf = nc.dram_tensor("d_xnf", [128, NT512 * 4096], F32, kind="ExternalOutput").ap()
        d_at = nc.dram_tensor("d_at", [TQ, C], F32, kind="ExternalOutput").ap()

    SB = nc.alloc_sbuf_tensor
    xn_own = SB("xn_own", [128, 8 * TQ], BF16)
    wq_bf = SB("wq_bf", [128, 8 * 256], BF16)
    wkd_bf = SB("wkd_bf", [128, 8 * 128], BF16)
    wv_bf = SB("wv_bf", [128, 8 * 64], BF16)
    wproj_bf = SB("wproj_bf", [128, 2 * C], BF16)
    wred_sb = SB("wred_sb", [128, 8 * GD], F32)
    b1r_sb = SB("b1r_sb", [128, E * 32], F32)
    maskP_bf = SB("maskP_bf", [128, 896], BF16)
    sel128_sb = SB("sel128_sb", [128, 128], F32)
    sel2_sb = SB("sel2_sb", [2, 128], F32)
    hmask2_sb = SB("hmask2_sb", [128, 2], F32)
    ident_sb = SB("ident_sb", [128, 128], F32)
    rot_sb = SB("rot_sb", [128, 128], F32)
    ones2_bf = SB("ones2_bf", [128, 2], BF16)
    onesr_bf = SB("onesr_bf", [1, 128], BF16)
    onesr_f32 = SB("onesr_f32", [1, 128], F32)
    msc_sb = SB("msc_sb", [128, 16 * E], F32)
    wg_sb = SB("wg_sb", [8, GD], F32)
    wgT_sb = SB("wgT_sb", [GD, E], F32)
    wgsq = SB("wgsq", [GD, E], F32)
    nrm = SB("nrm", [8, 1], F32)
    nrow = SB("nrow", [1, E], F32)
    wgnT_sb = SB("wgnT_sb", [GD, E], F32)

    lg_sb = SB("lg_sb", [128, 8], F32)
    nmaxv = SB("nmaxv", [128, 1], F32)
    maxv = SB("maxv", [128, 1], F32)
    esum = SB("esum", [128, 1], F32)
    rs2 = SB("rs2", [128, 1], F32)
    tb8 = SB("tb8", [128, 8], F32)
    tm4 = SB("tm4", [128, 4], F32)
    etr = SB("etr", [128, 8], F32)
    st_tm = SB("st_tm", [128, 8], F32)
    lnsum = SB("lnsum", [128, 1], F32)
    lnssq = SB("lnssq", [128, 1], F32)
    lnm2 = SB("lnm2", [128, 1], F32)
    lnvar = SB("lnvar", [128, 1], F32)


    rinv_sc = SB("rinv_sc", [128, 512], F32)
    rbsb = SB("rbsb", [128, 512], F32)
    rowsel_sb = SB("rowsel_sb", [2, 256], F32)
    st2_sb = SB("st2_sb", [2, 512], F32)
    epsln = SB("epsln", [128, 1], F32)
    epsq = SB("epsq", [2, 1], F32)
    epsk = SB("epsk", [2, 1], F32)
    srow = rinv_sc[0:2, 0:512]
    rrow = rbsb[0:2, 0:512]
    red_sb = rinv_sc[0:GD, 0:512]
    y_hp = SB("y_hp", [128, 2 * 512], BF16)

    segA = SB("segA", [128, 65536], U8)
    segQ = SB("segQ", [128, 32768], U8)
    segK = SB("segK", [128, 32768], U8)
    segX = SB("segX", [128, 16384], U8)
    if DBG:
        d_lgbuf = SB("d_lgbuf", [128, 16 * E], F32)
    b2e_bf = [SB("b2e0", [1, C], BF16), SB("b2e1", [1, C], BF16)]

    def view(seg, off, dtype, cols):
        nb = cols * (2 if dtype == BF16 else 4)
        return seg[:, off:off + nb].bitcast(dtype)

    xf_b = [view(segA, 0, F32, 4096), view(segA, 16384, F32, 4096)]
    xtmb = [view(segA, 32768, F32, 1024), view(segA, 36864, F32, 1024)]
    cs_b = [view(segA, 40960, F32, 1024), view(segA, 45056, F32, 1024)]
    xnf_b = [view(segA, 49152, BF16, 4096), view(segA, 57344, BF16, 4096)]
    acc = view(segA, 0, F32, 16384)

    q_fm = view(segQ, 0, BF16, 16384)
    w1_b = [view(segQ, 0, BF16, 8192), view(segQ, 16384, BF16, 8192)]
    ab1 = view(segQ, 0, F32, 8192)

    kd_fm = view(segK, 0, BF16, 8192)
    v_sb = view(segK, 16384, BF16, 4096)
    pt_b = [view(segK, 24576, BF16, 1024), view(segK, 26624, BF16, 1024)]
    atr = view(segK, 24576, F32, 1024)
    w2_b = [view(segK, 0, BF16, 8192), view(segK, 16384, BF16, 8192)]
    ab2 = view(segK, 0, F32, 8192)

    qsb_b = [view(segX, 0, F32, 512), view(segX, 2048, F32, 512)]
    qr_b = [view(segX, 4096, F32, 512), view(segX, 6144, F32, 512)]
    t1_s = view(segX, 8192, F32, 512)
    sq_s = view(segX, 10240, F32, 512)
    xnO_b = [view(segX, 12288, F32, 512), view(segX, 14336, F32, 512)]
    pp_acc = view(segX, 0, F32, 4096)
    h_b = [view(segX, 0, BF16, 2048), view(segX, 4096, BF16, 2048)]

    PS = nc.alloc_psum_tensor
    pair01 = PS("pair01", [128, 1024], F32)
    pair23 = PS("pair23", [128, 1024], F32)
    s4 = PS("s4", [128, 512], F32)
    s5 = PS("s5", [128, 512], F32)
    s6 = PS("s6", [128, 512], F32)
    s7 = PS("s7", [128, 512], F32)

    p = Plan(nc)
    for s in ("sd", "gd", "pe", "ac", "dv", "cc", "so", "sa", "gx", "sc", "sp"):
        p.add_sem(s)

    p.op("vector", lambda e: e.memset(epsln[:, :], LN_EPS))
    p.op("vector", lambda e: e.memset(epsq[:, :], 64.0 * RMS_EPS))
    p.op("vector", lambda e: e.memset(epsk[:, :], RMS_EPS), sig="dv", inc=1)
    d_eps = p.cur("dv")

    def dmac(eng, dst, src, waits=()):
        return p.op(eng, lambda e, d=dst, s=src: e.dma_start(out=d, in_=s),
                    waits=waits, sig=("sd" if eng == "sync" else "gd"), inc=16)

    dmac("gpsimd", wq_bf[:, :].rearrange("p (cc n) -> p cc n", n=256),
         wq.rearrange("(cc p) n -> p cc n", p=128))
    dmac("gpsimd", wkd_bf[:, :].rearrange("p (cc n) -> p cc n", n=128),
         wkd.rearrange("(cc p) n -> p cc n", p=128))
    dmac("gpsimd", wv_bf[:, :].rearrange("p (cc n) -> p cc n", n=64),
         wv.rearrange("(cc p) n -> p cc n", p=128))
    dmac("gpsimd", wproj_bf[:, :].rearrange("p (hp n) -> p hp n", n=C),
         wproj.rearrange("(hp p) n -> p hp n", p=128))
    dmac("gpsimd", maskP_bf[:, :], maskP)
    dmac("gpsimd", ones2_bf[:, :], onesc)
    g_w = dmac("gpsimd", onesr_bf[:, :], onesr)
    dmac("sync", wred_sb[:, :].rearrange("p (cc n) -> p cc n", n=GD),
         wred.rearrange("(cc p) n -> p cc n", p=128))
    dmac("sync", b1r_sb[:, :], b1r)
    dmac("sync", sel128_sb[:, :], sel128)
    dmac("sync", sel2_sb[:, :], sel2)
    dmac("sync", hmask2_sb[:, :], hmask2)
    dmac("sync", ident_sb[:, :], identf)
    dmac("sync", rot_sb[:, :], rotm)
    dmac("sync", onesr_f32[:, :], onesr)
    dmac("sync", rowsel_sb[:, :], rowsel)
    dmac("sync", wg_sb[:, :], wg)
    s_consts = dmac("sync", wgT_sb[:, :], wgT)

    for _eng in ("tensor", "vector", "scalar"):
        p.op(_eng, lambda e: e.nop(),
             waits=[("sd", s_consts), ("gd", g_w)])

    a = p.op("scalar", lambda e: e.activation(out=wgsq[:, :], in_=wgT_sb[:, :],
                                              func=AF.Square),
             waits=[("sd", s_consts)], sig="ac", inc=1)
    pe = p.op("tensor", lambda e: e.matmul(pair01[0:1, 0:E], hmask2_sb[0:GD, 0:1],
                                           wgsq[:, :], start=True, stop=True),
              waits=[("ac", a)], sig="pe", inc=1)
    a = p.op("scalar", lambda e: e.activation(out=nrow[:, :], in_=pair01[0:1, 0:E],
                                              func=AF.Sqrt),
             waits=[("pe", pe)], sig="ac", inc=1)
    d = p.op("vector", lambda e: e.tensor_scalar_max(nrow[:, :], nrow[:, :], 1e-4),
             waits=[("ac", a)], sig="dv", inc=1)
    d = p.op("vector", lambda e: e.reciprocal(nrow[:, :], nrow[:, :]),
             waits=[("dv", d)], sig="dv", inc=1)
    pe = p.op("tensor", lambda e: e.matmul(pair01[0:GD, 0:E], onesr_f32[0:1, 0:GD],
                                           nrow[:, :], start=True, stop=True),
              waits=[("dv", d)], sig="pe", inc=1)
    d_wgn = p.op("vector", lambda e: e.tensor_mul(wgnT_sb[:, :], wgT_sb[:, :],
                                                  pair01[0:GD, 0:E]),
                 waits=[("pe", pe)], sig="dv", inc=1)

    xf_free = [[], []]
    xtm_free = [[], []]
    cs_free = [[], []]
    xnf_free = [[], []]
    s45_free = [[("dv", d_wgn)], []]
    s6_free = []
    s7_free = []
    b0_free = []
    b1_free = []
    b23_free = []
    sttm_free = []
    stfm_free = []
    qsb_free = [[], []]
    mv_free = [[], [], [], []]
    qr_free = [[], []]
    xnO_free = [[], []]
    red_free = []
    lgmm_free = []
    lgsb_free = []

    def phaseA_tile(k, own):
        nonlocal s6_free, s7_free, b0_free, b1_free, b23_free
        nonlocal sttm_free, stfm_free, red_free, lgmm_free, lgsb_free
        par = k % 2
        src_fm = xoT if own else xT
        src_tm = xo if own else xtm

        s_xf = p.op("gpsimd", lambda e, k=k, src_fm=src_fm, par=par: e.dma_start(
            out=xf_b[par][:, :].rearrange("p (cc t) -> p cc t", t=512),
            in_=src_fm.rearrange("(cc p) t -> p cc t", p=128)[:, :, k * 512:(k + 1) * 512]),
            waits=xf_free[par] + [("gx", p.cur("gx"))], sig="gx", inc=16)
        if not own:
            p.op("sync", lambda e, k=k, par=par: e.dma_start(
                out=cs_b[par][:, 0:512], in_=cosf[:, k * 512:(k + 1) * 512]),
                waits=cs_free[par] + [("sc", p.cur("sc"))], sig="sc", inc=16)
            s_cs = p.op("sync", lambda e, k=k, par=par: e.dma_start(
                out=cs_b[par][:, 512:1024], in_=sinf[:, k * 512:(k + 1) * 512]),
                sig="sc", inc=16)
        else:
            s_cs = 0

        for sub in range(4):
            sp = sub % 2
            s_sub = p.op("scalar", lambda e, k=k, sub=sub, sp=sp, src_tm=src_tm: e.dma_start(
                out=xtmb[sp][:, :],
                in_=src_tm[k * 512 + sub * 128: k * 512 + (sub + 1) * 128, :]),
                waits=xtm_free[sp] + [("sa", p.cur("sa"))], sig="sa", inc=16)
            w = [("sa", s_sub)] + mv_free[sub] + (sttm_free if sub == 0 else [])
            p.op("scalar", lambda e, sp=sp: e.activation(
                out=atr[:, :], in_=xtmb[sp][:, :], func=AF.Square,
                accum_out=lnssq[:, :]), waits=w)
            a_sums = p.op("scalar", lambda e, sp=sp: e.activation(
                out=atr[:, :], in_=xtmb[sp][:, :], func=AF.Copy,
                accum_out=lnsum[:, :]), sig="ac", inc=1)
            xtm_free[sp] = [("ac", a_sums)]
            dd = p.op("vector", lambda e, sub=sub: e.tensor_scalar_mul(
                st_tm[:, 2 * sub:2 * sub + 1], lnsum[:, :], -1.0 / 1024.0),
                waits=[("ac", a_sums)], sig="dv", inc=1)
            dd = p.op("vector", lambda e, sub=sub: e.tensor_mul(
                lnm2[:, :], st_tm[:, 2 * sub:2 * sub + 1],
                st_tm[:, 2 * sub:2 * sub + 1]),
                waits=[("dv", dd)], sig="dv", inc=1)
            dd = p.op("vector", lambda e, dd=dd: e.scalar_tensor_tensor(
                out=lnvar[:, :], in0=lnssq[:, :], scalar=1.0 / 1024.0,
                in1=lnm2[:, :], op0=AluOpType.mult, op1=AluOpType.subtract),
                waits=[("dv", dd)], sig="dv", inc=1)
            aa = p.op("scalar", lambda e, sub=sub: e.activation(
                out=st_tm[:, 2 * sub + 1:2 * sub + 2], in_=lnvar[:, :],
                func=AF.Sqrt, bias=epsln[:, :]),
                waits=[("dv", dd)], sig="ac", inc=1)
            mv_free[sub] = [("ac", aa)]
            p.op("vector", lambda e, sub=sub: e.reciprocal(
                st_tm[:, 2 * sub + 1:2 * sub + 2], st_tm[:, 2 * sub + 1:2 * sub + 2]),
                waits=[("ac", aa)], sig="dv", inc=1)
        d_st = p.cur("dv")

        def trans4(e):
            ins = None
            for sub in range(4):
                ins = e.transpose(pair01[0:2, 512 + sub * 128: 640 + sub * 128],
                                  st_tm[:, 2 * sub:2 * sub + 2], ident_sb[:, :])
            return ins
        pe_tr = p.op("tensor", trans4,
                     waits=[("dv", d_st)] + b1_free + stfm_free, sig="pe", inc=1)
        sttm_free = [("pe", pe_tr)]
        a_st2 = p.op("scalar", lambda e: e.copy(st2_sb[:, :], pair01[0:2, 512:1024]),
                     waits=[("pe", pe_tr)] + stfm_free, sig="ac", inc=1)
        b1_free = [("ac", a_st2)]

        def bcasts(e):
            e.matmul(pair23[:, 0:512], rowsel_sb[:, 0:128], st2_sb[:, :],
                     start=True, stop=True)
            return e.matmul(pair23[:, 512:1024], rowsel_sb[:, 128:256],
                            st2_sb[:, :], start=True, stop=True)
        w_bc = [("ac", a_st2)] + b23_free
        if DBG and not own:
            s_dst = p.op("sync", lambda e, k=k: e.dma_start(
                out=d_stat[:, k * 512:(k + 1) * 512], in_=st2_sb[:, :]),
                waits=[("ac", a_st2)], sig="sd", inc=16)
            w_bc = w_bc + [("sd", s_dst)]
        pe_bc = p.op("tensor", bcasts, waits=w_bc, sig="pe", inc=1)
        stfm_free = [("pe", pe_bc)]

        d_last_norm = 0
        for cc in range(8):
            p.op("vector", lambda e, cc=cc, par=par: e.tensor_add(
                t1_s[:, :], xf_b[par][:, cc * 512:(cc + 1) * 512], pair23[:, 0:512]),
                waits=[("pe", pe_bc), ("gx", s_xf)])
            if own:
                op_buf = xnO_b[cc % 2]
                dd = p.op("vector", lambda e, op_buf=op_buf: e.tensor_mul(
                    op_buf[:, :], t1_s[:, :], pair23[:, 512:1024]),
                    waits=xnO_free[cc % 2], sig="dv", inc=1)
                aa = p.op("scalar", lambda e, cc=cc, k=k, op_buf=op_buf: e.copy(
                    xn_own[:, cc * TQ + k * 512: cc * TQ + (k + 1) * 512], op_buf[:, :]),
                    waits=[("dv", dd)], sig="ac", inc=1)
                w = [("dv", dd)] + ((s45_free[0] + red_free) if cc == 0 else [])
                pp = p.op("tensor", lambda e, cc=cc, op_buf=op_buf: e.matmul(
                    s4[0:GD, :], wred_sb[:, cc * GD:(cc + 1) * GD], op_buf[:, :],
                    start=(cc == 0), stop=(cc == 7)),
                    waits=w, sig="pe", inc=1)
                xnO_free[cc % 2] = [("ac", aa), ("pe", pp)]
                d_last_norm = dd
            else:
                dd = p.op("vector", lambda e, cc=cc, par=par: e.tensor_mul(
                    xnf_b[par][:, cc * 512:(cc + 1) * 512], t1_s[:, :],
                    pair23[:, 512:1024]),
                    waits=(xnf_free[par] if cc == 0 else []), sig="dv", inc=1)
                d_last_norm = dd
        b23_free = [("dv", d_last_norm)]
        xf_free[par] = [("dv", d_last_norm)]
        d_xn = d_last_norm

        if own:
            pe_red = p.cur("pe")
            a_red = p.op("scalar", lambda e: e.copy(red_sb[:, :], s4[0:GD, :]),
                         waits=[("pe", pe_red)] + red_free, sig="ac", inc=1)

            s45_free[0] = [("ac", a_red)]
            last_lg_pe = 0
            for sub in range(4):
                w = [("ac", a_red), ("dv", d_wgn)] + lgmm_free
                pe_lg = p.op("tensor", lambda e, sub=sub: e.matmul(
                    s5[0:128, 0:E], red_sb[:, sub * 128:(sub + 1) * 128],
                    wgnT_sb[:, :], start=True, stop=True),
                    waits=w, sig="pe", inc=1)
                last_lg_pe = pe_lg
                a_lg = p.op("scalar", lambda e: e.copy(lg_sb[:, :], s5[0:128, 0:E]),
                            waits=[("pe", pe_lg)] + lgsb_free, sig="ac", inc=1)
                lgmm_free = [("ac", a_lg)]
                dd = p.op("vector", lambda e: e.tensor_max(
                    tm4[:, :], lg_sb[:, 0:4], lg_sb[:, 4:8]),
                    waits=[("ac", a_lg)], sig="dv", inc=1)
                dd = p.op("vector", lambda e: e.tensor_max(
                    tm4[:, 0:2], tm4[:, 0:2], tm4[:, 2:4]),
                    waits=[("dv", dd)], sig="dv", inc=1)
                dd = p.op("vector", lambda e: e.tensor_max(
                    maxv[:, :], tm4[:, 0:1], tm4[:, 1:2]),
                    waits=[("dv", dd)], sig="dv", inc=1)
                d1 = p.op("vector", lambda e: e.tensor_scalar_mul(
                    nmaxv[:, :], maxv[:, :], -1.0),
                    waits=[("dv", dd)], sig="dv", inc=1)
                d_msct = p.op("vector", lambda e: e.tensor_single_scalar(
                    tb8[:, :], lg_sb[:, :], maxv[:, :], AluOpType.is_ge),
                    waits=[("dv", d1)], sig="dv", inc=1)
                a_ex = p.op("scalar", lambda e: e.activation(
                    out=etr[:, :], in_=lg_sb[:, :], func=AF.Exp,
                    bias=nmaxv[:, :], accum_out=esum[:, :]),
                    waits=[("dv", d1)], sig="ac", inc=1)
                d_rc = p.op("vector", lambda e: e.reciprocal(rs2[:, :], esum[:, :]),
                            waits=[("ac", a_ex)], sig="dv", inc=1)
                sg = k * 4 + sub
                a_msc = p.op("scalar", lambda e, sg=sg: e.activation(
                    out=msc_sb[:, sg * E:(sg + 1) * E], in_=tb8[:, :],
                    func=AF.Copy, scale=rs2[:, :]),
                    waits=[("dv", d_rc)], sig="ac", inc=1)
                if DBG:
                    p.op("vector", lambda e, sg=sg: e.tensor_copy(
                        d_lgbuf[:, sg * E:(sg + 1) * E], lg_sb[:, :]),
                        waits=[("ac", a_lg)])
                lgsb_free = [("dv", d_msct)]
            red_free = [("pe", last_lg_pe)]
            return

        for ci in range(3):
            ps, psi = (s4, 0) if ci % 2 == 0 else (s5, 1)
            wbuf, coff, blk = (wq_bf, ci * 128, 256) if ci < 2 else (wkd_bf, 0, 128)

            def mm_qkv(e, ps=ps, wbuf=wbuf, coff=coff, blk=blk, par=par):
                ins = None
                for cc in range(8):
                    ins = e.matmul(ps[:, :],
                                   wbuf[:, cc * blk + coff: cc * blk + coff + 128],
                                   xnf_b[par][:, cc * 512:(cc + 1) * 512],
                                   start=(cc == 0), stop=(cc == 7))
                return ins
            pe_qkv = p.op("tensor", mm_qkv,
                          waits=[("dv", d_xn), ("gd", g_w)] + s45_free[psi],
                          sig="pe", inc=1)
            qp = ci % 2
            a_ev = p.op("scalar", lambda e, ps=ps, qp=qp: e.copy(
                qsb_b[qp][:, :], ps[:, :]),
                waits=[("pe", pe_qkv)] + qsb_free[qp], sig="ac", inc=1)
            s45_free[psi] = [("ac", a_ev)]
            pe_rot = p.op("tensor", lambda e, qp=qp: e.matmul(
                s7[:, :], rot_sb[:, :], qsb_b[qp][:, :], start=True, stop=True),
                waits=[("ac", a_ev)] + s7_free, sig="pe", inc=1)
            d_t1 = p.op("vector", lambda e, qp=qp, par=par: e.tensor_mul(
                t1_s[:, :], qsb_b[qp][:, :], cs_b[par][:, 0:512]),
                waits=[("sc", s_cs), ("ac", a_ev)], sig="dv", inc=1)
            qsb_free[qp] = [("pe", pe_rot), ("dv", d_t1)]
            d_sq = p.op("vector", lambda e, par=par: e.tensor_mul(
                sq_s[:, :], s7[:, :], cs_b[par][:, 512:1024]),
                waits=[("pe", pe_rot)], sig="dv", inc=1)
            s7_free = [("dv", d_sq)]
            d_qr = p.op("vector", lambda e, qp=qp: e.tensor_add(
                qr_b[qp][:, :], t1_s[:, :], sq_s[:, :]),
                waits=qr_free[qp], sig="dv", inc=1)
            a_sq = p.op("scalar", lambda e, qp=qp: e.activation(
                out=sq_s[:, :], in_=qr_b[qp][:, :], func=AF.Square),
                waits=[("dv", d_qr)], sig="ac", inc=1)
            pe_ss = p.op("tensor", lambda e: e.matmul(
                pair01[0:2, 0:512], hmask2_sb[:, :], sq_s[:, :],
                start=True, stop=True),
                waits=[("ac", a_sq)] + b0_free, sig="pe", inc=1)
            scl = 1.0 if ci < 2 else 1.0 / 64.0
            bia = epsq if ci < 2 else epsk
            a_sr = p.op("scalar", lambda e, scl=scl, bia=bia: e.activation(
                out=srow, in_=pair01[0:2, 0:512], func=AF.Sqrt,
                bias=bia[:, :], scale=scl),
                waits=[("pe", pe_ss), ("dv", d_eps)], sig="ac", inc=1)
            b0_free = [("ac", a_sr)]
            d_rr = p.op("vector", lambda e: e.reciprocal(rrow, srow),
                        waits=[("ac", a_sr)], sig="dv", inc=1)
            pe_rb = p.op("tensor", lambda e: e.matmul(
                pair01[:, 512:1024], sel2_sb[:, :], rrow,
                start=True, stop=True),
                waits=[("dv", d_rr)] + b1_free, sig="pe", inc=1)
            if ci < 2:
                dst = q_fm[:, ci * T + k * 512: ci * T + (k + 1) * 512]
            else:
                dst = kd_fm[:, k * 512:(k + 1) * 512]
            d_fin = p.op("vector", lambda e, dst=dst, qp=qp: e.tensor_mul(
                dst, qr_b[qp][:, :], pair01[:, 512:1024]),
                waits=[("pe", pe_rb)], sig="dv", inc=1)
            b1_free = [("dv", d_fin)]
            qr_free[qp] = [("ac", a_sq), ("dv", d_fin)]

        def mm_v(e, par=par):
            ins = None
            for sub in range(4):
                for cc in range(8):
                    ins = e.matmul(
                        s6[:, sub * 64:(sub + 1) * 64],
                        xnf_b[par][:, cc * 512 + sub * 128: cc * 512 + (sub + 1) * 128],
                        wv_bf[:, cc * 64:(cc + 1) * 64],
                        start=(cc == 0 and sub == 0), stop=(cc == 7))
            return ins
        pe_v = p.op("tensor", mm_v, waits=[("dv", d_xn)] + s6_free,
                    sig="pe", inc=1)
        xnf_free[par] = [("pe", pe_v)]
        if DBG:
            s_dxn = p.op("gpsimd", lambda e, k=k, par=par: e.dma_start(
                out=d_xnf[:, k * 4096:(k + 1) * 4096], in_=xnf_b[par][:, :]),
                waits=[("dv", d_xn)], sig="gd", inc=16)
            xnf_free[par] = [("pe", pe_v), ("gd", s_dxn)]
        cs_free[par] = [("dv", p.cur("dv"))]
        a_v = p.op("scalar", lambda e, k=k: e.copy(
            v_sb[:, k * 256:(k + 1) * 256], s6[0:128, 0:256]),
            waits=[("pe", pe_v)], sig="ac", inc=1)
        s6_free = [("ac", a_v)]

    for k in range(NT512):
        phaseA_tile(k, own=False)
    for k in range(4):
        phaseA_tile(k, own=True)

    peA, acA, dvA = p.cur("pe"), p.cur("ac"), p.cur("dv")

    s_acc0 = p.op("sync", lambda e: e.dma_start(
        out=acc[:, :].rearrange("p (a c) -> p a c", c=C),
        in_=xo.rearrange("(a p) c -> p a c", p=128)),
        waits=[("pe", peA), ("ac", acA), ("dv", dvA)], sig="so", inc=16)

    pair_free = [[("ac", acA), ("dv", dvA)], [("ac", acA), ("dv", dvA)]]
    pt_free = [[], []]
    o_free = [("dv", dvA)]
    r_free = [("dv", dvA)]
    s7b_free = [("ac", acA)]
    rbsb_free = []
    yhp_free = [[], []]
    pp_bank_free = []
    pp_buf_free = []

    for wp in range(NT512):
        nsk = 4 * wp + 4
        for hp in range(2):
            pe_s_list = {}
            a_exp_list = {}

            def build_S(sk):
                spair = pair01 if sk % 2 == 0 else pair23
                spi = sk % 2

                def mm_s(e, spair=spair, sk=sk, hp=hp, wp=wp):
                    e.matmul(spair[:, 0:512],
                             kd_fm[0:64, sk * 128:(sk + 1) * 128],
                             q_fm[0:64, hp * T + wp * 512: hp * T + (wp + 1) * 512],
                             start=True, stop=True, tile_position=(0, 0))
                    return e.matmul(spair[:, 512:1024],
                                    kd_fm[64:128, sk * 128:(sk + 1) * 128],
                                    q_fm[64:128, hp * T + wp * 512: hp * T + (wp + 1) * 512],
                                    start=True, stop=True, tile_position=(64, 0))
                pe_s_list[sk] = p.op("tensor", mm_s, waits=pair_free[spi],
                                     sig="pe", inc=1)

            def build_exp(sk):
                spair = pair01 if sk % 2 == 0 else pair23
                ptb = pt_b[sk % 2]
                a_exp = p.op("scalar", lambda e, spair=spair, ptb=ptb: e.activation(
                    out=ptb[:, :], in_=spair[:, :], func=AF.Exp),
                    waits=[("pe", pe_s_list[sk])] + pt_free[sk % 2],
                    sig="ac", inc=1)
                pair_free[sk % 2] = [("ac", a_exp)]
                a_exp_list[sk] = a_exp

            build_S(0)
            for sk in range(nsk):
                ptb = pt_b[sk % 2]
                build_exp(sk)
                if sk + 1 < nsk:
                    build_S(sk + 1)
                a_exp = a_exp_list[sk]
                dtile = sk - (nsk - 4)
                if dtile >= 0:
                    off = (3 - dtile) * 128

                    def mm_mask(e, ptb=ptb, off=off):
                        e.tensor_tensor(ptb[:, 0:512], ptb[:, 0:512],
                                        maskP_bf[:, off:off + 512],
                                        op=AluOpType.mult)
                        return e.tensor_tensor(ptb[:, 512:1024], ptb[:, 512:1024],
                                               maskP_bf[:, off:off + 512],
                                               op=AluOpType.mult)
                    d_m = p.op("vector", mm_mask, waits=[("ac", a_exp)],
                               sig="dv", inc=1)
                    w_pv = [("dv", d_m)]
                else:
                    w_pv = [("ac", a_exp)]
                if sk == 0:
                    w_pv = w_pv + o_free + r_free

                def mm_v2(e, ptb=ptb, sk=sk, first=(sk == 0), last=(sk == nsk - 1)):
                    e.matmul(s4[0:64, :], v_sb[:, sk * 64:(sk + 1) * 64],
                             ptb[:, 0:512], start=first, stop=last,
                             tile_position=(0, 0))
                    e.matmul(s4[64:128, :], v_sb[:, sk * 64:(sk + 1) * 64],
                             ptb[:, 512:1024], start=first, stop=last,
                             tile_position=(0, 64))
                    e.matmul(s6[0:1, :], ones2_bf[:, 0:1], ptb[:, 0:512],
                             start=first, stop=last, tile_position=(0, 0))
                    return e.matmul(s6[64:65, :], ones2_bf[:, 1:2],
                                    ptb[:, 512:1024], start=first, stop=last,
                                    tile_position=(0, 64))
                pe_pv = p.op("tensor", mm_v2, waits=w_pv, sig="pe", inc=1)
                pt_free[sk % 2] = [("pe", pe_pv)]

            pe_last = p.cur("pe")
            p.op("vector", lambda e: e.reciprocal(rinv_sc[0:1, :], s6[0:1, :]),
                 waits=[("pe", pe_last)])
            d_ri = p.op("vector", lambda e: e.reciprocal(
                rinv_sc[64:65, :], s6[64:65, :]), sig="dv", inc=1)
            r_free = [("dv", d_ri)]
            pe_rb = p.op("tensor", lambda e: e.matmul(
                s7[:, :], sel128_sb[:, :], rinv_sc[:, :], start=True, stop=True),
                waits=[("dv", d_ri)] + s7b_free, sig="pe", inc=1)
            a_rb = p.op("scalar", lambda e: e.copy(rbsb[:, :], s7[:, :]),
                        waits=[("pe", pe_rb)] + rbsb_free, sig="ac", inc=1)
            s7b_free = [("ac", a_rb)]
            d_y = p.op("vector", lambda e, hp=hp: e.tensor_mul(
                y_hp[:, hp * 512:(hp + 1) * 512], s4[:, :], rbsb[:, :]),
                waits=[("ac", a_rb)] + yhp_free[hp], sig="dv", inc=1)
            if DBG:
                s_dy = p.op("gpsimd", lambda e, wp=wp, hp=hp: e.dma_start(
                    out=d_yhp[(wp * 2 + hp) * 128:(wp * 2 + hp + 1) * 128, :],
                    in_=y_hp[:, hp * 512:(hp + 1) * 512]),
                    waits=[("dv", d_y)], sig="gd", inc=16)
                yhp_free[hp] = yhp_free[hp] + [("gd", s_dy)]
            o_free = [("dv", d_y)]
            rbsb_free = [("dv", d_y)]
            for sub in range(4):
                for half in range(2):
                    def mm_pp(e, sub=sub, half=half, hp=hp):
                        return e.matmul(
                            s5[:, :],
                            y_hp[:, hp * 512 + sub * 128: hp * 512 + (sub + 1) * 128],
                            wproj_bf[:, hp * C + half * 512: hp * C + (half + 1) * 512],
                            start=True, stop=True)
                    w = [("dv", d_y)] + pp_bank_free
                    if hp == 0 and sub == 0 and half == 0:
                        w = w + pp_buf_free
                    pe_pp = p.op("tensor", mm_pp, waits=w, sig="pe", inc=1)
                    dst = pp_acc[:, sub * C + half * 512: sub * C + (half + 1) * 512]
                    if hp == 0:
                        a_e = p.op("scalar", lambda e, dst=dst: e.copy(dst, s5[:, :]),
                                   waits=[("pe", pe_pp)], sig="ac", inc=1)
                        pp_bank_free = [("ac", a_e)]
                    else:
                        d_e = p.op("vector", lambda e, dst=dst: e.tensor_add(
                            dst, dst, s5[:, :]),
                            waits=[("pe", pe_pp)], sig="dv", inc=1)
                        pp_bank_free = [("dv", d_e)]
            yhp_free[hp] = [("pe", pe_pp)]
        d_win = p.cur("dv")
        s_pp = p.op("sync", lambda e, wp=wp: e.dma_start(
            out=pp_dram[wp * 512:(wp + 1) * 512, :].rearrange("(a p) c -> p a c", p=128),
            in_=pp_acc[:, :].rearrange("p (a c) -> p a c", c=C)),
            waits=[("dv", d_win), ("sp", p.cur("sp"))], sig="sp", inc=16)
        pp_buf_free = [("sp", s_pp)]

    s_ppall = p.cur("sp")
    peB, acB, dvB = p.cur("pe"), p.cur("ac"), p.cur("dv")
    if DBG:
        p.op("gpsimd", lambda e: e.dma_start(out=d_xn, in_=xn_own[:, :]),
             waits=[("pe", peB)], sig="gd", inc=16)
        p.op("gpsimd", lambda e: e.dma_start(out=d_q, in_=q_fm[:, :]),
             sig="gd", inc=16)
        p.op("gpsimd", lambda e: e.dma_start(out=d_k, in_=kd_fm[:, :]),
             sig="gd", inc=16)
        p.op("gpsimd", lambda e: e.dma_start(out=d_v, in_=v_sb[:, :]),
             sig="gd", inc=16)
        p.op("gpsimd", lambda e: e.dma_start(out=d_msc, in_=msc_sb[:, :]),
             sig="gd", inc=16)
        p.op("gpsimd", lambda e: e.dma_start(out=d_lg, in_=d_lgbuf[:, :]),
             sig="gd", inc=16)
        p.op("gpsimd", lambda e: e.dma_start(out=dd_wgn, in_=wgnT_sb[:, :]),
             sig="gd", inc=16)

        p.op("gpsimd", lambda e: e.dma_start(out=d_pp, in_=pp_dram),
             waits=[("sp", s_ppall)], sig="gd", inc=16)
        dbg_gd = p.cur("gd")
        peB = ("dbg", peB)

    p.op("gpsimd", lambda e: e.collective_compute(
        "ReduceScatter", AluOpType.add,
        replica_groups=[[0, 1, 2, 3], [4, 5, 6, 7]],
        ins=[pp_dram.opt()], outs=[attn_own.opt()]),
        waits=[("sp", s_ppall)], sig="cc", inc=1)

    NFB = 4
    blocks = [(e_i, fb) for e_i in range(E) for fb in range(NFB)]

    def wdma(n, extra_waits):
        e_i, fb = blocks[n]
        par = n % 2
        p.op("gpsimd", lambda e, e_i=e_i, fb=fb, par=par: e.dma_start(
            out=w1_b[par][:, :].rearrange("p (cc f) -> p cc f", f=1024),
            in_=w1[e_i, :, fb * 1024:(fb + 1) * 1024].rearrange(
                "(cc p) f -> p cc f", p=128)),
            waits=extra_waits + [("gx", p.cur("gx"))], sig="gx", inc=16)
        return p.op("gpsimd", lambda e, e_i=e_i, fb=fb, par=par: e.dma_start(
            out=w2_b[par][:, :].rearrange("p (fc c) -> p fc c", c=1024),
            in_=w2[e_i, fb * 1024:(fb + 1) * 1024, :].rearrange(
                "(fc p) c -> p fc c", p=128)),
            sig="gx", inc=16)

    if DBG:
        peB = peB[1]
    g_blk = {}
    extra0 = [("gd", dbg_gd)] if DBG else []
    g_blk[0] = wdma(0, [("pe", peB), ("ac", acB), ("dv", dvB)] + extra0)
    g_blk[1] = wdma(1, [])
    b2_loaded = {}
    b2_free = {0: [("pe", peB)], 1: [("pe", peB)]}
    if DBG:
        p.op("gpsimd", lambda e: e.dma_start(out=d_at, in_=attn_own),
             waits=[("cc", 1)], sig="gd", inc=16)
    def b2dma(e_i):
        bpar = e_i % 2
        b2_loaded[e_i] = p.op("gpsimd", lambda e, e_i=e_i, bpar=bpar: e.dma_start(
            out=b2e_bf[bpar][0:1, :], in_=b2c[0:1, e_i * C:(e_i + 1) * C]),
            waits=b2_free[bpar], sig="gd", inc=16)
    b2dma(0)
    b2dma(1)

    hps_free = {0: [("ac", acB)], 1: [("ac", acB)]}
    hbuf_free = [[("dv", dvB)], [("dv", dvB)]]
    ypair_free = {0: [("dv", dvB), ("so", s_acc0)], 1: [("dv", dvB), ("so", s_acc0)]}
    last_pe_of_par = {0: 0, 1: 0}

    for n, (e_i, fb) in enumerate(blocks):
        par = n % 2
        gwait = g_blk[n]
        for tg in range(8):
            hpar = tg % 2
            a_g_list = {}

            def build_y(fc):
                for sub in range(2):
                    ypair = pair01 if sub == 0 else pair23
                    stop_here = (fc == 7 and fb != NFB - 1)

                    def mm_y(e, par=par, fc=fc, sub=sub, ypair=ypair, hpar=hpar,
                             first=(fc == 0), stop_here=stop_here):
                        e.matmul(ypair[:, 0:512],
                                 h_b[hpar][:, fc * 256 + sub * 128: fc * 256 + sub * 128 + 128],
                                 w2_b[par][:, fc * 1024: fc * 1024 + 512],
                                 start=first, stop=stop_here)
                        return e.matmul(ypair[:, 512:1024],
                                        h_b[hpar][:, fc * 256 + sub * 128: fc * 256 + sub * 128 + 128],
                                        w2_b[par][:, fc * 1024 + 512: fc * 1024 + 1024],
                                        start=first, stop=stop_here)
                    w = [("ac", a_g_list[fc])]
                    if fc == 0:
                        w = w + ypair_free[sub]
                    p.op("tensor", mm_y, waits=w, sig="pe", inc=1)

            for fc in range(8):
                hps, hpi = (s4, 0) if fc % 2 == 0 else (s5, 1)

                def mm_h(e, par=par, fc=fc, tg=tg, hps=hps):
                    ins = None
                    for cc in range(8):
                        ins = e.matmul(
                            hps[:, 0:256],
                            w1_b[par][:, cc * 1024 + fc * 128: cc * 1024 + (fc + 1) * 128],
                            xn_own[:, cc * TQ + tg * 256: cc * TQ + (tg + 1) * 256],
                            start=(cc == 0), stop=(cc == 7))
                    return ins
                pe_h = p.op("tensor", mm_h, waits=[("gx", gwait)] + hps_free[hpi],
                            sig="pe", inc=1)
                bcol = e_i * 32 + fb * 8 + fc
                a_g = p.op("scalar", lambda e, hps=hps, hpar=hpar, fc=fc, bcol=bcol: e.activation(
                    out=h_b[hpar][:, fc * 256:(fc + 1) * 256],
                    in_=hps[:, 0:256], func=AF.Gelu,
                    bias=b1r_sb[:, bcol:bcol + 1]),
                    waits=[("pe", pe_h)] + (hbuf_free[hpar] if fc == 0 else []),
                    sig="ac", inc=1)
                hps_free[hpi] = [("ac", a_g)]
                a_g_list[fc] = a_g
                if fc > 0:
                    build_y(fc - 1)
            build_y(7)
            hbuf_free[hpar] = [("pe", p.cur("pe"))]
            for sub in range(2):
                ypair = pair01 if sub == 0 else pair23
                if fb == NFB - 1:
                    bpar = e_i % 2
                    pe_y = p.op("tensor", lambda e, bpar=bpar, ypair=ypair: (
                        e.matmul(ypair[:, 0:512], onesr_bf[:, :],
                                 b2e_bf[bpar][0:1, 0:512],
                                 start=False, stop=True),
                        e.matmul(ypair[:, 512:1024], onesr_bf[:, :],
                                 b2e_bf[bpar][0:1, 512:C],
                                 start=False, stop=True))[-1],
                        waits=[("sa", b2_loaded[e_i])], sig="pe", inc=1)
                else:
                    pe_y = p.cur("pe")
                sg = tg * 2 + sub

                def stt(e, e_i=e_i, sg=sg, ypair=ypair):
                    e.scalar_tensor_tensor(
                        out=acc[:, sg * C: sg * C + 512],
                        in0=ypair[:, 0:512],
                        scalar=msc_sb[:, sg * E + e_i: sg * E + e_i + 1],
                        in1=acc[:, sg * C: sg * C + 512],
                        op0=AluOpType.mult, op1=AluOpType.add)
                    return e.scalar_tensor_tensor(
                        out=acc[:, sg * C + 512:(sg + 1) * C],
                        in0=ypair[:, 512:1024],
                        scalar=msc_sb[:, sg * E + e_i: sg * E + e_i + 1],
                        in1=acc[:, sg * C + 512:(sg + 1) * C],
                        op0=AluOpType.mult, op1=AluOpType.add)
                d_s = p.op("vector", stt, waits=[("pe", pe_y)], sig="dv", inc=1)
                ypair_free[sub] = [("dv", d_s)]
                if DBG and e_i == 0 and fb == 0 and tg == 0 and sub == 1:
                    gdum = p.op("gpsimd", lambda e: e.dma_start(out=d_acc0, in_=acc[:, :]),
                                waits=[("dv", d_s)], sig="gd", inc=16)
                    ypair_free[0] = ypair_free[0] + [("gd", gdum)]
                    ypair_free[1] = ypair_free[1] + [("gd", gdum)]
        if DBG and e_i == 0 and fb == 0 and tg == 7 and False:
            pass
        last_pe_of_par[par] = p.cur("pe")

        if fb == NFB - 1:
            b2_free[e_i % 2] = [("pe", p.cur("pe"))]
            if e_i + 2 < E:
                b2dma(e_i + 2)
        if n + 2 < len(blocks):
            g_blk[n + 2] = wdma(n + 2, [("pe", last_pe_of_par[par])])

    pe_endC = p.cur("pe")

    g1 = p.op("gpsimd", lambda e: e.dma_start(
        out=ab1[:, :].rearrange("p (a c) -> p a c", c=C),
        in_=attn_own[0:1024, :].rearrange("(a p) c -> p a c", p=128)),
        waits=[("cc", 1), ("pe", pe_endC)], sig="gd", inc=16)
    g2 = p.op("gpsimd", lambda e: e.dma_start(
        out=ab2[:, :].rearrange("p (a c) -> p a c", c=C),
        in_=attn_own[1024:2048, :].rearrange("(a p) c -> p a c", p=128)),
        sig="gd", inc=16)
    d_pre = p.cur("dv")
    for sg in range(16):
        src = ab1 if sg < 8 else ab2
        so = (sg % 8) * C
        p.op("vector", lambda e, sg=sg, src=src, so=so: e.tensor_add(
            acc[:, sg * C:(sg + 1) * C], acc[:, sg * C:(sg + 1) * C],
            src[:, so:so + C]),
            waits=[("gd", g2), ("dv", d_pre)], sig="dv", inc=1)
    d_fin = p.cur("dv")
    p.op("sync", lambda e: e.dma_start(
        out=out.rearrange("(a p) c -> p a c", p=128),
        in_=acc[:, :].rearrange("p (a c) -> p a c", c=C)),
        waits=[("dv", d_fin)], sig="so", inc=16)
    p.op("sync", lambda e: e.nop(), waits=[("so", p.cur("so"))])

    with nc.Block() as block:
        @block.sync
        def _(e):
            p.emit("sync", e)

        @block.gpsimd
        def _(e):
            p.emit("gpsimd", e)

        @block.tensor
        def _(e):
            p.emit("tensor", e)

        @block.vector
        def _(e):
            p.emit("vector", e)

        @block.scalar
        def _(e):
            p.emit("scalar", e)

    return nc



def _rope_perm():
    return np.array([2 * i for i in range(32)] + [2 * i + 1 for i in range(32)])


def _constants():
    f32 = np.float32
    pos = np.arange(T, dtype=f32)
    inv_freq = (1.0 / np.float32(10000.0) ** (np.arange(0, HD, 2, dtype=f32) / np.float32(HD))).astype(f32)
    freqs = pos[:, None] * inv_freq[None, :]
    cosT = np.cos(freqs).astype(f32)
    sinT = np.sin(freqs).astype(f32)
    cosf = np.empty((128, T), dtype=f32)
    sinf = np.empty((128, T), dtype=f32)
    for pp in range(128):
        cosf[pp] = cosT[:, pp % 32]
        sinf[pp] = sinT[:, pp % 32]
    rotm = np.zeros((128, 128), dtype=f32)
    for b0 in (0, 64):
        for i in range(32):
            rotm[b0 + i + 32, b0 + i] = -1.0
            rotm[b0 + i, b0 + i + 32] = 1.0
    sel128 = np.zeros((128, 128), dtype=f32)
    sel128[0, 0:64] = 1.0
    sel128[64, 64:128] = 1.0
    sel2 = np.zeros((2, 128), dtype=f32)
    sel2[0, 0:64] = 1.0
    sel2[1, 64:128] = 1.0
    hmask2 = np.zeros((128, 2), dtype=f32)
    hmask2[0:64, 0] = 1.0
    hmask2[64:128, 1] = 1.0
    identf = np.eye(128, dtype=f32)
    s_i = np.arange(128)[:, None]
    u_i = np.arange(896)[None, :]
    maskP = (u_i >= s_i + 384).astype(f32)
    onesc = np.ones((128, 2), dtype=f32)
    rowsel = np.zeros((2, 256), dtype=f32)
    rowsel[0, 0:128] = 1.0
    rowsel[1, 128:256] = 1.0
    onesr = np.ones((1, 128), dtype=f32)
    return dict(cosf=cosf, sinf=sinf, rotm=rotm, sel128=sel128, sel2=sel2,
                hmask2=hmask2, identf=identf, maskP=maskP, onesc=onesc,
                onesr=onesr, rowsel=rowsel)


def kernel(L0_attn_x, L0_attn_w_attn, L0_attn_w_proj, L0_ffn_w_red, L0_ffn_wg,
           L0_ffn_w1, L0_ffn_b1, L0_ffn_w2, L0_ffn_b2):
    f32 = np.float32
    x = np.asarray(L0_attn_x, dtype=f32)
    w_attn = np.asarray(L0_attn_w_attn, dtype=f32)
    w_proj = np.asarray(L0_attn_w_proj, dtype=f32)
    w_red = np.ascontiguousarray(np.asarray(L0_ffn_w_red, dtype=f32))
    wg = np.ascontiguousarray(np.asarray(L0_ffn_wg, dtype=f32))
    w1 = np.ascontiguousarray(np.asarray(L0_ffn_w1, dtype=f32))
    b1 = np.asarray(L0_ffn_b1, dtype=f32)
    w2 = np.ascontiguousarray(np.asarray(L0_ffn_w2, dtype=f32))
    b2 = np.asarray(L0_ffn_b2, dtype=f32)

    perm = _rope_perm()
    consts = _constants()
    b1r = np.ascontiguousarray(b1.reshape(E, 32, 128).transpose(2, 0, 1).reshape(128, E * 32))
    b2c = np.ascontiguousarray(b2.reshape(1, E * C))
    wgT = np.ascontiguousarray(wg.T)

    in_maps = []
    for core in range(8):
        b = core // 4
        g = core % 4
        xb = x[b]
        xT_a = np.ascontiguousarray(xb.T)
        xo_a = np.ascontiguousarray(xb[g * TQ:(g + 1) * TQ])
        xoT_a = np.ascontiguousarray(xo_a.T)
        qcols = []
        for h in range(4 * g, 4 * g + 4):
            qcols.extend((h * HD + perm).tolist())
        wq_a = np.ascontiguousarray(w_attn[:, qcols])
        kcols = (C + g * HD + perm).tolist()
        wkd_a = np.ascontiguousarray(w_attn[:, kcols + kcols])
        vbase = C + NKV * HD + g * HD
        wv_a = np.ascontiguousarray(w_attn[:, vbase:vbase + HD])
        wproj_a = np.ascontiguousarray(w_proj[4 * g * HD:(4 * g + 4) * HD, :])
        m = dict(xT=xT_a, xtm=xb, xoT=xoT_a, xo=xo_a, wq=wq_a, wkd=wkd_a,
                 wv=wv_a, wproj=wproj_a, wred=w_red, wg=wg, wgT=wgT, w1=w1,
                 b1r=b1r, w2=w2, b2c=b2c, **consts)
        in_maps.append(m)

    nc = build_program()
    res = run_bass_kernel_spmd(nc, in_maps, core_ids=list(range(8)))
    if DBG:
        global DBG_RES
        DBG_RES = res.results
    outp = np.empty((B, T, C), dtype=f32)
    for core in range(8):
        b = core // 4
        g = core % 4
        outp[b, g * TQ:(g + 1) * TQ] = res.results[core]["out"]
    return outp


# version 07 (final)
# speedup vs baseline: 1.0337x; 1.0337x over previous
import numpy as np
import concourse.bass as bass
import concourse.mybir as mybir
from concourse.bass_utils import run_bass_kernel_spmd
from concourse.alu_op_type import AluOpType

F32 = mybir.dt.float32
BF16 = mybir.dt.bfloat16
U8 = mybir.dt.uint8
AF = mybir.ActivationFunctionType
AX = mybir.AxisListType.X

B, T, C = 2, 8192, 1024
NH, NKV, HD = 16, 4, 64
E, FF, GD = 8, 4096, 16
TQ = 2048
NT512 = 16
LN_EPS = 1e-5
RMS_EPS = 1e-6
DBG = False
DBG_RES = None


class Plan:

    def __init__(self, nc):
        self.nc = nc
        self.ops = {k: [] for k in ("sync", "gpsimd", "tensor", "vector", "scalar")}
        self.sem = {}
        self.count = {}

    def add_sem(self, name):
        self.sem[name] = self.nc.semaphore(name).__enter__()
        self.count[name] = 0

    def op(self, eng, fn, waits=(), sig=None, inc=0):
        ws = [(s, v) for (s, v) in waits if v > 0]
        if sig is not None:
            self.count[sig] += inc
        self.ops[eng].append((fn, ws, sig, inc))
        return self.count[sig] if sig is not None else None

    def cur(self, sem):
        return self.count[sem]

    def emit(self, eng_name, eng):
        for fn, waits, sig, inc in self.ops[eng_name]:
            best = {}
            for (s, v) in waits:
                best[s] = max(best.get(s, 0), v)
            for s, v in best.items():
                eng.wait_ge(self.sem[s], v)
            ins = fn(eng)
            if sig is not None and ins is not None:
                ins.then_inc(self.sem[sig], inc)


def build_program():
    nc = bass.Bass("TRN2", target_bir_lowering=False, debug=False)

    def din(name, shape):
        return nc.dram_tensor(name, shape, F32, kind="ExternalInput").ap()

    xT = din("xT", [C, T])
    xtm = din("xtm", [T, C])
    xoT = din("xoT", [C, TQ])
    xo = din("xo", [TQ, C])
    wq = din("wq", [C, 256])
    wkd = din("wkd", [C, 128])
    wv = din("wv", [C, 64])
    wproj = din("wproj", [256, C])
    wred = din("wred", [C, GD])
    wg = din("wg", [E, GD])
    wgT = din("wgT", [GD, E])
    w1 = din("w1", [E, C, FF])
    b1r = din("b1r", [128, E * 32])
    w2 = din("w2", [E, FF, C])
    b2c = din("b2c", [1, E * C])
    cosf = din("cosf", [128, T])
    sinf = din("sinf", [128, T])
    rotm = din("rotm", [128, 128])
    sel128 = din("sel128", [128, 128])
    sel2 = din("sel2", [2, 128])
    hmask2 = din("hmask2", [128, 2])
    identf = din("identf", [128, 128])
    maskP = din("maskP", [128, 896])
    rowsel = din("rowsel", [2, 256])
    onesc = din("onesc", [128, 2])
    onesr = din("onesr", [1, 128])

    pp_dram = nc.dram_tensor("pp_dram", [T, C], F32, kind="Internal").ap()
    attn_own = nc.dram_tensor("attn_own", [TQ, C], F32, kind="Internal").ap()
    vecb = nc.dram_tensor("vecb", [E, 1], F32, kind="Internal").ap()
    out = nc.dram_tensor("out", [TQ, C], F32, kind="ExternalOutput").ap()
    if DBG:
        d_xn = nc.dram_tensor("d_xn", [128, 8 * TQ], F32, kind="ExternalOutput").ap()
        d_q = nc.dram_tensor("d_q", [128, 16384], F32, kind="ExternalOutput").ap()
        d_k = nc.dram_tensor("d_k", [128, 8192], F32, kind="ExternalOutput").ap()
        d_v = nc.dram_tensor("d_v", [128, 4096], F32, kind="ExternalOutput").ap()
        d_msc = nc.dram_tensor("d_msc", [128, 128], F32, kind="ExternalOutput").ap()
        d_pp = nc.dram_tensor("d_pp", [T, C], F32, kind="ExternalOutput").ap()
        d_stat = nc.dram_tensor("d_stat", [2, NT512 * 512], F32, kind="ExternalOutput").ap()
        d_lg = nc.dram_tensor("d_lg", [128, 16 * E], F32, kind="ExternalOutput").ap()
        dd_wgn = nc.dram_tensor("d_wgn", [GD, E], F32, kind="ExternalOutput").ap()
        d_red = nc.dram_tensor("d_red", [GD, 512], F32, kind="ExternalOutput").ap()
        d_yhp = nc.dram_tensor("d_yhp", [32 * 128, 512], F32, kind="ExternalOutput").ap()
        d_acc0 = nc.dram_tensor("d_acc0", [128, 16384], F32, kind="ExternalOutput").ap()
        d_h0 = nc.dram_tensor("d_h0", [128, 2048], F32, kind="ExternalOutput").ap()
        d_xnf = nc.dram_tensor("d_xnf", [128, NT512 * 4096], F32, kind="ExternalOutput").ap()
        d_at = nc.dram_tensor("d_at", [TQ, C], F32, kind="ExternalOutput").ap()

    SB = nc.alloc_sbuf_tensor
    xn_own = SB("xn_own", [128, 8 * TQ], BF16)
    wq_bf = SB("wq_bf", [128, 8 * 256], BF16)
    wkd_bf = SB("wkd_bf", [128, 8 * 128], BF16)
    wv_bf = SB("wv_bf", [128, 8 * 64], BF16)
    wproj_bf = SB("wproj_bf", [128, 2 * C], BF16)
    wred_sb = SB("wred_sb", [128, 8 * GD], F32)
    b1r_sb = SB("b1r_sb", [128, E * 32], F32)
    maskP_bf = SB("maskP_bf", [128, 896], BF16)
    sel128_sb = SB("sel128_sb", [128, 128], F32)
    sel2_sb = SB("sel2_sb", [2, 128], F32)
    hmask2_sb = SB("hmask2_sb", [128, 2], F32)
    ident_sb = SB("ident_sb", [128, 128], F32)
    rot_sb = SB("rot_sb", [128, 128], F32)
    ones2_bf = SB("ones2_bf", [128, 2], BF16)
    onesr_bf = SB("onesr_bf", [1, 128], BF16)
    onesr_f32 = SB("onesr_f32", [1, 128], F32)
    msc_sb = SB("msc_sb", [128, 16 * E], F32)
    wg_sb = SB("wg_sb", [8, GD], F32)
    wgT_sb = SB("wgT_sb", [GD, E], F32)
    wgsq = SB("wgsq", [GD, E], F32)
    nrm = SB("nrm", [8, 1], F32)
    nrow = SB("nrow", [1, E], F32)
    wgnT_sb = SB("wgnT_sb", [GD, E], F32)

    lg_sb = SB("lg_sb", [128, 8], F32)
    nmaxv = SB("nmaxv", [128, 1], F32)
    maxv = SB("maxv", [128, 1], F32)
    esum = SB("esum", [128, 1], F32)
    rs2 = SB("rs2", [128, 1], F32)
    tb8 = SB("tb8", [128, 8], F32)
    tm4 = SB("tm4", [128, 4], F32)
    etr = SB("etr", [128, 8], F32)
    st_tm = SB("st_tm", [128, 8], F32)
    lnsum = SB("lnsum", [128, 1], F32)
    lnssq = SB("lnssq", [128, 1], F32)
    lnm2 = SB("lnm2", [128, 1], F32)
    lnvar = SB("lnvar", [128, 1], F32)


    rinv_sc = SB("rinv_sc", [128, 512], F32)
    rbsb = SB("rbsb", [128, 512], F32)
    rowsel_sb = SB("rowsel_sb", [2, 256], F32)
    st2_sb = SB("st2_sb", [2, 512], F32)
    epsln = SB("epsln", [128, 1], F32)
    epsq = SB("epsq", [2, 1], F32)
    epsk = SB("epsk", [2, 1], F32)
    srow = rinv_sc[0:2, 0:512]
    rrow = rbsb[0:2, 0:512]
    red_sb = rinv_sc[0:GD, 0:512]
    y_hp = SB("y_hp", [128, 2 * 512], BF16)

    segA = SB("segA", [128, 65536], U8)
    segQ = SB("segQ", [128, 32768], U8)
    segK = SB("segK", [128, 32768], U8)
    segX = SB("segX", [128, 16384], U8)
    if DBG:
        d_lgbuf = SB("d_lgbuf", [128, 16 * E], F32)
    b2e_bf = [SB("b2e0", [1, C], BF16), SB("b2e1", [1, C], BF16)]

    def view(seg, off, dtype, cols):
        nb = cols * (2 if dtype == BF16 else 4)
        return seg[:, off:off + nb].bitcast(dtype)

    xf_b = [view(segA, 0, F32, 4096), view(segA, 16384, F32, 4096)]
    xtmb = [view(segA, 32768, F32, 1024), view(segA, 36864, F32, 1024)]
    cs_b = [view(segA, 40960, F32, 1024), view(segA, 45056, F32, 1024)]
    xnf_b = [view(segA, 49152, BF16, 4096), view(segA, 57344, BF16, 4096)]
    acc = view(segA, 0, F32, 16384)

    q_fm = view(segQ, 0, BF16, 16384)
    w1_b = [view(segQ, 0, BF16, 8192), view(segQ, 16384, BF16, 8192)]
    ab1 = view(segQ, 0, F32, 8192)

    kd_fm = view(segK, 0, BF16, 8192)
    v_sb = view(segK, 16384, BF16, 4096)
    pt_b = [view(segK, 24576, BF16, 1024), view(segK, 26624, BF16, 1024)]
    atr = view(segK, 24576, F32, 1024)
    w2_b = [view(segK, 0, BF16, 8192), view(segK, 16384, BF16, 8192)]
    ab2 = view(segK, 0, F32, 8192)

    qsb_b = [view(segX, 0, F32, 512), view(segX, 2048, F32, 512)]
    qr_b = [view(segX, 4096, F32, 512), view(segX, 6144, F32, 512)]
    t1_s = view(segX, 8192, F32, 512)
    sq_s = view(segX, 10240, F32, 512)
    xnO_b = [view(segX, 12288, F32, 512), view(segX, 14336, F32, 512)]
    pp_acc = view(segX, 0, F32, 4096)
    h_b = [view(segX, 0, BF16, 2048), view(segX, 4096, BF16, 2048)]

    PS = nc.alloc_psum_tensor
    pair01 = PS("pair01", [128, 1024], F32)
    pair23 = PS("pair23", [128, 1024], F32)
    s4 = PS("s4", [128, 512], F32)
    s5 = PS("s5", [128, 512], F32)
    s6 = PS("s6", [128, 512], F32)
    s7 = PS("s7", [128, 512], F32)

    p = Plan(nc)
    for s in ("sd", "gd", "pe", "ac", "dv", "cc", "so", "sa", "gx", "sc", "sp"):
        p.add_sem(s)

    p.op("vector", lambda e: e.memset(epsln[:, :], LN_EPS))
    p.op("vector", lambda e: e.memset(epsq[:, :], 64.0 * RMS_EPS))
    p.op("vector", lambda e: e.memset(epsk[:, :], RMS_EPS), sig="dv", inc=1)
    d_eps = p.cur("dv")

    def dmac(eng, dst, src, waits=()):
        return p.op(eng, lambda e, d=dst, s=src: e.dma_start(out=d, in_=s),
                    waits=waits, sig=("sd" if eng == "sync" else "gd"), inc=16)

    dmac("gpsimd", wq_bf[:, :].rearrange("p (cc n) -> p cc n", n=256),
         wq.rearrange("(cc p) n -> p cc n", p=128))
    dmac("gpsimd", wkd_bf[:, :].rearrange("p (cc n) -> p cc n", n=128),
         wkd.rearrange("(cc p) n -> p cc n", p=128))
    dmac("gpsimd", wv_bf[:, :].rearrange("p (cc n) -> p cc n", n=64),
         wv.rearrange("(cc p) n -> p cc n", p=128))
    dmac("gpsimd", wproj_bf[:, :].rearrange("p (hp n) -> p hp n", n=C),
         wproj.rearrange("(hp p) n -> p hp n", p=128))
    dmac("gpsimd", maskP_bf[:, :], maskP)
    dmac("gpsimd", ones2_bf[:, :], onesc)
    g_w = dmac("gpsimd", onesr_bf[:, :], onesr)
    dmac("sync", wred_sb[:, :].rearrange("p (cc n) -> p cc n", n=GD),
         wred.rearrange("(cc p) n -> p cc n", p=128))
    dmac("sync", b1r_sb[:, :], b1r)
    dmac("sync", sel128_sb[:, :], sel128)
    dmac("sync", sel2_sb[:, :], sel2)
    dmac("sync", hmask2_sb[:, :], hmask2)
    dmac("sync", ident_sb[:, :], identf)
    dmac("sync", rot_sb[:, :], rotm)
    dmac("sync", onesr_f32[:, :], onesr)
    dmac("sync", rowsel_sb[:, :], rowsel)
    dmac("sync", wg_sb[:, :], wg)
    s_consts = dmac("sync", wgT_sb[:, :], wgT)

    for _eng in ("tensor", "vector", "scalar"):
        p.op(_eng, lambda e: e.nop(),
             waits=[("sd", s_consts), ("gd", g_w)])

    a = p.op("scalar", lambda e: e.activation(out=wgsq[:, :], in_=wgT_sb[:, :],
                                              func=AF.Square),
             waits=[("sd", s_consts)], sig="ac", inc=1)
    pe = p.op("tensor", lambda e: e.matmul(pair01[0:1, 0:E], hmask2_sb[0:GD, 0:1],
                                           wgsq[:, :], start=True, stop=True),
              waits=[("ac", a)], sig="pe", inc=1)
    a = p.op("scalar", lambda e: e.activation(out=nrow[:, :], in_=pair01[0:1, 0:E],
                                              func=AF.Sqrt),
             waits=[("pe", pe)], sig="ac", inc=1)
    d = p.op("vector", lambda e: e.tensor_scalar_max(nrow[:, :], nrow[:, :], 1e-4),
             waits=[("ac", a)], sig="dv", inc=1)
    d = p.op("vector", lambda e: e.reciprocal(nrow[:, :], nrow[:, :]),
             waits=[("dv", d)], sig="dv", inc=1)
    pe = p.op("tensor", lambda e: e.matmul(pair01[0:GD, 0:E], onesr_f32[0:1, 0:GD],
                                           nrow[:, :], start=True, stop=True),
              waits=[("dv", d)], sig="pe", inc=1)
    d_wgn = p.op("vector", lambda e: e.tensor_mul(wgnT_sb[:, :], wgT_sb[:, :],
                                                  pair01[0:GD, 0:E]),
                 waits=[("pe", pe)], sig="dv", inc=1)

    xf_free = [[], []]
    xtm_free = [[], []]
    cs_free = [[], []]
    xnf_free = [[], []]
    s45_free = [[("dv", d_wgn)], []]
    s6_free = []
    s7_free = []
    b0_free = []
    b1_free = []
    b23_free = []
    sttm_free = []
    stfm_free = []
    qsb_free = [[], []]
    mv_free = [[], [], [], []]
    qr_free = [[], []]
    xnO_free = [[], []]
    red_free = []
    lgmm_free = []
    lgsb_free = []

    def phaseA_tile(k, own):
        nonlocal s6_free, s7_free, b0_free, b1_free, b23_free
        nonlocal sttm_free, stfm_free, red_free, lgmm_free, lgsb_free
        par = k % 2
        src_fm = xoT if own else xT
        src_tm = xo if own else xtm

        s_xf = p.op("gpsimd", lambda e, k=k, src_fm=src_fm, par=par: e.dma_start(
            out=xf_b[par][:, :].rearrange("p (cc t) -> p cc t", t=512),
            in_=src_fm.rearrange("(cc p) t -> p cc t", p=128)[:, :, k * 512:(k + 1) * 512]),
            waits=xf_free[par] + [("gx", p.cur("gx"))], sig="gx", inc=16)
        if not own:
            p.op("sync", lambda e, k=k, par=par: e.dma_start(
                out=cs_b[par][:, 0:512], in_=cosf[:, k * 512:(k + 1) * 512]),
                waits=cs_free[par] + [("sc", p.cur("sc"))], sig="sc", inc=16)
            s_cs = p.op("sync", lambda e, k=k, par=par: e.dma_start(
                out=cs_b[par][:, 512:1024], in_=sinf[:, k * 512:(k + 1) * 512]),
                sig="sc", inc=16)
        else:
            s_cs = 0

        for sub in range(4):
            sp = sub % 2
            s_sub = p.op("scalar", lambda e, k=k, sub=sub, sp=sp, src_tm=src_tm: e.dma_start(
                out=xtmb[sp][:, :],
                in_=src_tm[k * 512 + sub * 128: k * 512 + (sub + 1) * 128, :]),
                waits=xtm_free[sp] + [("sa", p.cur("sa"))], sig="sa", inc=16)
            w = [("sa", s_sub)] + mv_free[sub] + (sttm_free if sub == 0 else [])
            p.op("scalar", lambda e, sp=sp: e.activation(
                out=atr[:, :], in_=xtmb[sp][:, :], func=AF.Square,
                accum_out=lnssq[:, :]), waits=w)
            a_sums = p.op("scalar", lambda e, sp=sp: e.activation(
                out=atr[:, :], in_=xtmb[sp][:, :], func=AF.Copy,
                accum_out=lnsum[:, :]), sig="ac", inc=1)
            xtm_free[sp] = [("ac", a_sums)]
            dd = p.op("vector", lambda e, sub=sub: e.tensor_scalar_mul(
                st_tm[:, 2 * sub:2 * sub + 1], lnsum[:, :], -1.0 / 1024.0),
                waits=[("ac", a_sums)], sig="dv", inc=1)
            dd = p.op("vector", lambda e, sub=sub: e.tensor_mul(
                lnm2[:, :], st_tm[:, 2 * sub:2 * sub + 1],
                st_tm[:, 2 * sub:2 * sub + 1]),
                waits=[("dv", dd)], sig="dv", inc=1)
            dd = p.op("vector", lambda e, dd=dd: e.scalar_tensor_tensor(
                out=lnvar[:, :], in0=lnssq[:, :], scalar=1.0 / 1024.0,
                in1=lnm2[:, :], op0=AluOpType.mult, op1=AluOpType.subtract),
                waits=[("dv", dd)], sig="dv", inc=1)
            aa = p.op("scalar", lambda e, sub=sub: e.activation(
                out=st_tm[:, 2 * sub + 1:2 * sub + 2], in_=lnvar[:, :],
                func=AF.Sqrt, bias=epsln[:, :]),
                waits=[("dv", dd)], sig="ac", inc=1)
            mv_free[sub] = [("ac", aa)]
            p.op("vector", lambda e, sub=sub: e.reciprocal(
                st_tm[:, 2 * sub + 1:2 * sub + 2], st_tm[:, 2 * sub + 1:2 * sub + 2]),
                waits=[("ac", aa)], sig="dv", inc=1)
        d_st = p.cur("dv")

        def trans4(e):
            ins = None
            for sub in range(4):
                ins = e.transpose(pair01[0:2, 512 + sub * 128: 640 + sub * 128],
                                  st_tm[:, 2 * sub:2 * sub + 2], ident_sb[:, :])
            return ins
        pe_tr = p.op("tensor", trans4,
                     waits=[("dv", d_st)] + b1_free + stfm_free, sig="pe", inc=1)
        sttm_free = [("pe", pe_tr)]
        a_st2 = p.op("scalar", lambda e: e.copy(st2_sb[:, :], pair01[0:2, 512:1024]),
                     waits=[("pe", pe_tr)] + stfm_free, sig="ac", inc=1)
        b1_free = [("ac", a_st2)]

        def bcasts(e):
            e.matmul(pair23[:, 0:512], rowsel_sb[:, 0:128], st2_sb[:, :],
                     start=True, stop=True)
            return e.matmul(pair23[:, 512:1024], rowsel_sb[:, 128:256],
                            st2_sb[:, :], start=True, stop=True)
        w_bc = [("ac", a_st2)] + b23_free
        if DBG and not own:
            s_dst = p.op("sync", lambda e, k=k: e.dma_start(
                out=d_stat[:, k * 512:(k + 1) * 512], in_=st2_sb[:, :]),
                waits=[("ac", a_st2)], sig="sd", inc=16)
            w_bc = w_bc + [("sd", s_dst)]
        pe_bc = p.op("tensor", bcasts, waits=w_bc, sig="pe", inc=1)
        stfm_free = [("pe", pe_bc)]

        d_last_norm = 0
        for cc in range(8):
            p.op("vector", lambda e, cc=cc, par=par: e.tensor_add(
                t1_s[:, :], xf_b[par][:, cc * 512:(cc + 1) * 512], pair23[:, 0:512]),
                waits=[("pe", pe_bc), ("gx", s_xf)])
            if own:
                op_buf = xnO_b[cc % 2]
                dd = p.op("vector", lambda e, op_buf=op_buf: e.tensor_mul(
                    op_buf[:, :], t1_s[:, :], pair23[:, 512:1024]),
                    waits=xnO_free[cc % 2], sig="dv", inc=1)
                aa = p.op("scalar", lambda e, cc=cc, k=k, op_buf=op_buf: e.copy(
                    xn_own[:, cc * TQ + k * 512: cc * TQ + (k + 1) * 512], op_buf[:, :]),
                    waits=[("dv", dd)], sig="ac", inc=1)
                w = [("dv", dd)] + ((s45_free[0] + red_free) if cc == 0 else [])
                pp = p.op("tensor", lambda e, cc=cc, op_buf=op_buf: e.matmul(
                    s4[0:GD, :], wred_sb[:, cc * GD:(cc + 1) * GD], op_buf[:, :],
                    start=(cc == 0), stop=(cc == 7)),
                    waits=w, sig="pe", inc=1)
                xnO_free[cc % 2] = [("ac", aa), ("pe", pp)]
                d_last_norm = dd
            else:
                dd = p.op("vector", lambda e, cc=cc, par=par: e.tensor_mul(
                    xnf_b[par][:, cc * 512:(cc + 1) * 512], t1_s[:, :],
                    pair23[:, 512:1024]),
                    waits=(xnf_free[par] if cc == 0 else []), sig="dv", inc=1)
                d_last_norm = dd
        b23_free = [("dv", d_last_norm)]
        xf_free[par] = [("dv", d_last_norm)]
        d_xn = d_last_norm

        if own:
            pe_red = p.cur("pe")
            a_red = p.op("scalar", lambda e: e.copy(red_sb[:, :], s4[0:GD, :]),
                         waits=[("pe", pe_red)] + red_free, sig="ac", inc=1)

            s45_free[0] = [("ac", a_red)]
            last_lg_pe = 0
            for sub in range(4):
                w = [("ac", a_red), ("dv", d_wgn)] + lgmm_free
                pe_lg = p.op("tensor", lambda e, sub=sub: e.matmul(
                    s5[0:128, 0:E], red_sb[:, sub * 128:(sub + 1) * 128],
                    wgnT_sb[:, :], start=True, stop=True),
                    waits=w, sig="pe", inc=1)
                last_lg_pe = pe_lg
                a_lg = p.op("scalar", lambda e: e.copy(lg_sb[:, :], s5[0:128, 0:E]),
                            waits=[("pe", pe_lg)] + lgsb_free, sig="ac", inc=1)
                lgmm_free = [("ac", a_lg)]
                dd = p.op("vector", lambda e: e.tensor_max(
                    tm4[:, :], lg_sb[:, 0:4], lg_sb[:, 4:8]),
                    waits=[("ac", a_lg)], sig="dv", inc=1)
                dd = p.op("vector", lambda e: e.tensor_max(
                    tm4[:, 0:2], tm4[:, 0:2], tm4[:, 2:4]),
                    waits=[("dv", dd)], sig="dv", inc=1)
                dd = p.op("vector", lambda e: e.tensor_max(
                    maxv[:, :], tm4[:, 0:1], tm4[:, 1:2]),
                    waits=[("dv", dd)], sig="dv", inc=1)
                d1 = p.op("vector", lambda e: e.tensor_scalar_mul(
                    nmaxv[:, :], maxv[:, :], -1.0),
                    waits=[("dv", dd)], sig="dv", inc=1)
                d_msct = p.op("vector", lambda e: e.tensor_single_scalar(
                    tb8[:, :], lg_sb[:, :], maxv[:, :], AluOpType.is_ge),
                    waits=[("dv", d1)], sig="dv", inc=1)
                a_ex = p.op("scalar", lambda e: e.activation(
                    out=etr[:, :], in_=lg_sb[:, :], func=AF.Exp,
                    bias=nmaxv[:, :], accum_out=esum[:, :]),
                    waits=[("dv", d1)], sig="ac", inc=1)
                d_rc = p.op("vector", lambda e: e.reciprocal(rs2[:, :], esum[:, :]),
                            waits=[("ac", a_ex)], sig="dv", inc=1)
                sg = k * 4 + sub
                a_msc = p.op("scalar", lambda e, sg=sg: e.activation(
                    out=msc_sb[:, sg * E:(sg + 1) * E], in_=tb8[:, :],
                    func=AF.Copy, scale=rs2[:, :]),
                    waits=[("dv", d_rc)], sig="ac", inc=1)
                if DBG:
                    p.op("vector", lambda e, sg=sg: e.tensor_copy(
                        d_lgbuf[:, sg * E:(sg + 1) * E], lg_sb[:, :]),
                        waits=[("ac", a_lg)])
                lgsb_free = [("dv", d_msct)]
            red_free = [("pe", last_lg_pe)]
            return

        for ci in range(3):
            ps, psi = (s4, 0) if ci % 2 == 0 else (s5, 1)
            wbuf, coff, blk = (wq_bf, ci * 128, 256) if ci < 2 else (wkd_bf, 0, 128)

            def mm_qkv(e, ps=ps, wbuf=wbuf, coff=coff, blk=blk, par=par):
                ins = None
                for cc in range(8):
                    ins = e.matmul(ps[:, :],
                                   wbuf[:, cc * blk + coff: cc * blk + coff + 128],
                                   xnf_b[par][:, cc * 512:(cc + 1) * 512],
                                   start=(cc == 0), stop=(cc == 7))
                return ins
            pe_qkv = p.op("tensor", mm_qkv,
                          waits=[("dv", d_xn), ("gd", g_w)] + s45_free[psi],
                          sig="pe", inc=1)
            qp = ci % 2
            a_ev = p.op("scalar", lambda e, ps=ps, qp=qp: e.copy(
                qsb_b[qp][:, :], ps[:, :]),
                waits=[("pe", pe_qkv)] + qsb_free[qp], sig="ac", inc=1)
            s45_free[psi] = [("ac", a_ev)]
            pe_rot = p.op("tensor", lambda e, qp=qp: e.matmul(
                s7[:, :], rot_sb[:, :], qsb_b[qp][:, :], start=True, stop=True),
                waits=[("ac", a_ev)] + s7_free, sig="pe", inc=1)
            d_t1 = p.op("vector", lambda e, qp=qp, par=par: e.tensor_mul(
                t1_s[:, :], qsb_b[qp][:, :], cs_b[par][:, 0:512]),
                waits=[("sc", s_cs), ("ac", a_ev)], sig="dv", inc=1)
            qsb_free[qp] = [("pe", pe_rot), ("dv", d_t1)]
            d_sq = p.op("vector", lambda e, par=par: e.tensor_mul(
                sq_s[:, :], s7[:, :], cs_b[par][:, 512:1024]),
                waits=[("pe", pe_rot)], sig="dv", inc=1)
            s7_free = [("dv", d_sq)]
            d_qr = p.op("vector", lambda e, qp=qp: e.tensor_add(
                qr_b[qp][:, :], t1_s[:, :], sq_s[:, :]),
                waits=qr_free[qp], sig="dv", inc=1)
            a_sq = p.op("scalar", lambda e, qp=qp: e.activation(
                out=sq_s[:, :], in_=qr_b[qp][:, :], func=AF.Square),
                waits=[("dv", d_qr)], sig="ac", inc=1)
            pe_ss = p.op("tensor", lambda e: e.matmul(
                pair01[0:2, 0:512], hmask2_sb[:, :], sq_s[:, :],
                start=True, stop=True),
                waits=[("ac", a_sq)] + b0_free, sig="pe", inc=1)
            scl = 1.0 if ci < 2 else 1.0 / 64.0
            bia = epsq if ci < 2 else epsk
            a_sr = p.op("scalar", lambda e, scl=scl, bia=bia: e.activation(
                out=srow, in_=pair01[0:2, 0:512], func=AF.Sqrt,
                bias=bia[:, :], scale=scl),
                waits=[("pe", pe_ss), ("dv", d_eps)], sig="ac", inc=1)
            b0_free = [("ac", a_sr)]
            d_rr = p.op("vector", lambda e: e.reciprocal(rrow, srow),
                        waits=[("ac", a_sr)], sig="dv", inc=1)
            pe_rb = p.op("tensor", lambda e: e.matmul(
                pair01[:, 512:1024], sel2_sb[:, :], rrow,
                start=True, stop=True),
                waits=[("dv", d_rr)] + b1_free, sig="pe", inc=1)
            if ci < 2:
                dst = q_fm[:, ci * T + k * 512: ci * T + (k + 1) * 512]
            else:
                dst = kd_fm[:, k * 512:(k + 1) * 512]
            d_fin = p.op("vector", lambda e, dst=dst, qp=qp: e.tensor_mul(
                dst, qr_b[qp][:, :], pair01[:, 512:1024]),
                waits=[("pe", pe_rb)], sig="dv", inc=1)
            b1_free = [("dv", d_fin)]
            qr_free[qp] = [("ac", a_sq), ("dv", d_fin)]

        def mm_v(e, par=par):
            ins = None
            for sub in range(4):
                for cc in range(8):
                    ins = e.matmul(
                        s6[:, sub * 64:(sub + 1) * 64],
                        xnf_b[par][:, cc * 512 + sub * 128: cc * 512 + (sub + 1) * 128],
                        wv_bf[:, cc * 64:(cc + 1) * 64],
                        start=(cc == 0 and sub == 0), stop=(cc == 7))
            return ins
        pe_v = p.op("tensor", mm_v, waits=[("dv", d_xn)] + s6_free,
                    sig="pe", inc=1)
        xnf_free[par] = [("pe", pe_v)]
        if DBG:
            s_dxn = p.op("gpsimd", lambda e, k=k, par=par: e.dma_start(
                out=d_xnf[:, k * 4096:(k + 1) * 4096], in_=xnf_b[par][:, :]),
                waits=[("dv", d_xn)], sig="gd", inc=16)
            xnf_free[par] = [("pe", pe_v), ("gd", s_dxn)]
        cs_free[par] = [("dv", p.cur("dv"))]
        a_v = p.op("scalar", lambda e, k=k: e.copy(
            v_sb[:, k * 256:(k + 1) * 256], s6[0:128, 0:256]),
            waits=[("pe", pe_v)], sig="ac", inc=1)
        s6_free = [("ac", a_v)]

    for k in range(NT512):
        phaseA_tile(k, own=False)
    for k in range(4):
        phaseA_tile(k, own=True)

    peA, acA, dvA = p.cur("pe"), p.cur("ac"), p.cur("dv")

    s_acc0 = p.op("sync", lambda e: e.dma_start(
        out=acc[:, :].rearrange("p (a c) -> p a c", c=C),
        in_=xo.rearrange("(a p) c -> p a c", p=128)),
        waits=[("pe", peA), ("ac", acA), ("dv", dvA)], sig="so", inc=16)

    pair_free = [[("ac", acA), ("dv", dvA)], [("ac", acA), ("dv", dvA)]]
    pt_free = [[], []]
    o_free = [("dv", dvA)]
    r_free = [("dv", dvA)]
    s7b_free = [("ac", acA)]
    rbsb_free = []
    yhp_free = [[], []]
    pp_bank_free = {0: [], 1: []}
    pp_buf_free = []

    for wp in range(NT512):
        nsk = 4 * wp + 4
        for hp in range(2):
            pe_s_list = {}
            a_exp_list = {}

            def build_S(sk):
                spair = pair01 if sk % 2 == 0 else pair23
                spi = sk % 2

                def mm_s(e, spair=spair, sk=sk, hp=hp, wp=wp):
                    e.matmul(spair[:, 0:512],
                             kd_fm[0:64, sk * 128:(sk + 1) * 128],
                             q_fm[0:64, hp * T + wp * 512: hp * T + (wp + 1) * 512],
                             start=True, stop=True, tile_position=(0, 0))
                    return e.matmul(spair[:, 512:1024],
                                    kd_fm[64:128, sk * 128:(sk + 1) * 128],
                                    q_fm[64:128, hp * T + wp * 512: hp * T + (wp + 1) * 512],
                                    start=True, stop=True, tile_position=(64, 0))
                pe_s_list[sk] = p.op("tensor", mm_s, waits=pair_free[spi],
                                     sig="pe", inc=1)

            def build_exp(sk):
                spair = pair01 if sk % 2 == 0 else pair23
                ptb = pt_b[sk % 2]
                a_exp = p.op("scalar", lambda e, spair=spair, ptb=ptb: e.activation(
                    out=ptb[:, :], in_=spair[:, :], func=AF.Exp),
                    waits=[("pe", pe_s_list[sk])] + pt_free[sk % 2],
                    sig="ac", inc=1)
                pair_free[sk % 2] = [("ac", a_exp)]
                a_exp_list[sk] = a_exp

            build_S(0)
            for sk in range(nsk):
                ptb = pt_b[sk % 2]
                build_exp(sk)
                if sk + 1 < nsk:
                    build_S(sk + 1)
                a_exp = a_exp_list[sk]
                dtile = sk - (nsk - 4)
                if dtile >= 0:
                    off = (3 - dtile) * 128

                    def mm_mask(e, ptb=ptb, off=off):
                        e.tensor_tensor(ptb[:, 0:512], ptb[:, 0:512],
                                        maskP_bf[:, off:off + 512],
                                        op=AluOpType.mult)
                        return e.tensor_tensor(ptb[:, 512:1024], ptb[:, 512:1024],
                                               maskP_bf[:, off:off + 512],
                                               op=AluOpType.mult)
                    d_m = p.op("vector", mm_mask, waits=[("ac", a_exp)],
                               sig="dv", inc=1)
                    w_pv = [("dv", d_m)]
                else:
                    w_pv = [("ac", a_exp)]
                if sk == 0:
                    w_pv = w_pv + o_free + r_free

                def mm_v2(e, ptb=ptb, sk=sk, first=(sk == 0), last=(sk == nsk - 1)):
                    e.matmul(s4[0:64, :], v_sb[:, sk * 64:(sk + 1) * 64],
                             ptb[:, 0:512], start=first, stop=last,
                             tile_position=(0, 0))
                    e.matmul(s4[64:128, :], v_sb[:, sk * 64:(sk + 1) * 64],
                             ptb[:, 512:1024], start=first, stop=last,
                             tile_position=(0, 64))
                    e.matmul(s6[0:1, :], ones2_bf[:, 0:1], ptb[:, 0:512],
                             start=first, stop=last, tile_position=(0, 0))
                    return e.matmul(s6[64:65, :], ones2_bf[:, 1:2],
                                    ptb[:, 512:1024], start=first, stop=last,
                                    tile_position=(0, 64))
                pe_pv = p.op("tensor", mm_v2, waits=w_pv, sig="pe", inc=1)
                pt_free[sk % 2] = [("pe", pe_pv)]

            pe_last = p.cur("pe")
            p.op("vector", lambda e: e.reciprocal(rinv_sc[0:1, :], s6[0:1, :]),
                 waits=[("pe", pe_last)])
            d_ri = p.op("vector", lambda e: e.reciprocal(
                rinv_sc[64:65, :], s6[64:65, :]), sig="dv", inc=1)
            r_free = [("dv", d_ri)]
            pe_rb = p.op("tensor", lambda e: e.matmul(
                s7[:, :], sel128_sb[:, :], rinv_sc[:, :], start=True, stop=True),
                waits=[("dv", d_ri)] + s7b_free + pp_bank_free[1], sig="pe", inc=1)
            a_rb = p.op("scalar", lambda e: e.copy(rbsb[:, :], s7[:, :]),
                        waits=[("pe", pe_rb)] + rbsb_free, sig="ac", inc=1)
            s7b_free = [("ac", a_rb)]
            d_y = p.op("vector", lambda e, hp=hp: e.tensor_mul(
                y_hp[:, hp * 512:(hp + 1) * 512], s4[:, :], rbsb[:, :]),
                waits=[("ac", a_rb)] + yhp_free[hp], sig="dv", inc=1)
            if DBG:
                s_dy = p.op("gpsimd", lambda e, wp=wp, hp=hp: e.dma_start(
                    out=d_yhp[(wp * 2 + hp) * 128:(wp * 2 + hp + 1) * 128, :],
                    in_=y_hp[:, hp * 512:(hp + 1) * 512]),
                    waits=[("dv", d_y)], sig="gd", inc=16)
                yhp_free[hp] = yhp_free[hp] + [("gd", s_dy)]
            o_free = [("dv", d_y)]
            rbsb_free = [("dv", d_y)]
            pp_bank_free[1] = pp_bank_free[1] + [("ac", a_rb)]
            for j in range(8):
                sub, half = j // 2, j % 2
                bank = s5 if j % 2 == 0 else s7

                def mm_pp(e, sub=sub, half=half, hp=hp, bank=bank):
                    return e.matmul(
                        bank[:, :],
                        y_hp[:, hp * 512 + sub * 128: hp * 512 + (sub + 1) * 128],
                        wproj_bf[:, hp * C + half * 512: hp * C + (half + 1) * 512],
                        start=True, stop=True)
                w = [("dv", d_y)] + pp_bank_free[j % 2]
                if hp == 0 and sub == 0 and half == 0:
                    w = w + pp_buf_free
                pe_pp = p.op("tensor", mm_pp, waits=w, sig="pe", inc=1)
                dst = pp_acc[:, sub * C + half * 512: sub * C + (half + 1) * 512]
                if hp == 0:
                    a_e = p.op("scalar", lambda e, dst=dst, bank=bank: e.copy(dst, bank[:, :]),
                               waits=[("pe", pe_pp)], sig="ac", inc=1)
                    pp_bank_free[j % 2] = [("ac", a_e)]
                else:
                    d_e = p.op("vector", lambda e, dst=dst, bank=bank: e.tensor_add(
                        dst, dst, bank[:, :]),
                        waits=[("pe", pe_pp)], sig="dv", inc=1)
                    pp_bank_free[j % 2] = [("dv", d_e)]
            yhp_free[hp] = [("pe", pe_pp)]
        d_win = p.cur("dv")
        s_pp = p.op("sync", lambda e, wp=wp: e.dma_start(
            out=pp_dram[wp * 512:(wp + 1) * 512, :].rearrange("(a p) c -> p a c", p=128),
            in_=pp_acc[:, :].rearrange("p (a c) -> p a c", c=C)),
            waits=[("dv", d_win), ("sp", p.cur("sp"))], sig="sp", inc=16)
        pp_buf_free = [("sp", s_pp)]

    s_ppall = p.cur("sp")
    peB, acB, dvB = p.cur("pe"), p.cur("ac"), p.cur("dv")
    if DBG:
        p.op("gpsimd", lambda e: e.dma_start(out=d_xn, in_=xn_own[:, :]),
             waits=[("pe", peB)], sig="gd", inc=16)
        p.op("gpsimd", lambda e: e.dma_start(out=d_q, in_=q_fm[:, :]),
             sig="gd", inc=16)
        p.op("gpsimd", lambda e: e.dma_start(out=d_k, in_=kd_fm[:, :]),
             sig="gd", inc=16)
        p.op("gpsimd", lambda e: e.dma_start(out=d_v, in_=v_sb[:, :]),
             sig="gd", inc=16)
        p.op("gpsimd", lambda e: e.dma_start(out=d_msc, in_=msc_sb[:, :]),
             sig="gd", inc=16)
        p.op("gpsimd", lambda e: e.dma_start(out=d_lg, in_=d_lgbuf[:, :]),
             sig="gd", inc=16)
        p.op("gpsimd", lambda e: e.dma_start(out=dd_wgn, in_=wgnT_sb[:, :]),
             sig="gd", inc=16)

        p.op("gpsimd", lambda e: e.dma_start(out=d_pp, in_=pp_dram),
             waits=[("sp", s_ppall)], sig="gd", inc=16)
        dbg_gd = p.cur("gd")
        peB = ("dbg", peB)

    p.op("gpsimd", lambda e: e.collective_compute(
        "ReduceScatter", AluOpType.add,
        replica_groups=[[0, 1, 2, 3], [4, 5, 6, 7]],
        ins=[pp_dram.opt()], outs=[attn_own.opt()]),
        waits=[("sp", s_ppall)], sig="cc", inc=1)

    NFB = 4
    blocks = [(e_i, fb) for e_i in range(E) for fb in range(NFB)]

    def wdma(n, extra_waits):
        e_i, fb = blocks[n]
        par = n % 2
        p.op("gpsimd", lambda e, e_i=e_i, fb=fb, par=par: e.dma_start(
            out=w1_b[par][:, :].rearrange("p (cc f) -> p cc f", f=1024),
            in_=w1[e_i, :, fb * 1024:(fb + 1) * 1024].rearrange(
                "(cc p) f -> p cc f", p=128)),
            waits=extra_waits + [("gx", p.cur("gx"))], sig="gx", inc=16)
        return p.op("gpsimd", lambda e, e_i=e_i, fb=fb, par=par: e.dma_start(
            out=w2_b[par][:, :].rearrange("p (fc c) -> p fc c", c=1024),
            in_=w2[e_i, fb * 1024:(fb + 1) * 1024, :].rearrange(
                "(fc p) c -> p fc c", p=128)),
            sig="gx", inc=16)

    if DBG:
        peB = peB[1]
    g_blk = {}
    extra0 = [("gd", dbg_gd)] if DBG else []
    g_blk[0] = wdma(0, [("pe", peB), ("ac", acB), ("dv", dvB)] + extra0)
    g_blk[1] = wdma(1, [])
    b2_loaded = {}
    b2_free = {0: [("pe", peB)], 1: [("pe", peB)]}
    if DBG:
        p.op("gpsimd", lambda e: e.dma_start(out=d_at, in_=attn_own),
             waits=[("cc", 1)], sig="gd", inc=16)
    def b2dma(e_i):
        bpar = e_i % 2
        b2_loaded[e_i] = p.op("gpsimd", lambda e, e_i=e_i, bpar=bpar: e.dma_start(
            out=b2e_bf[bpar][0:1, :], in_=b2c[0:1, e_i * C:(e_i + 1) * C]),
            waits=b2_free[bpar], sig="gd", inc=16)
    b2dma(0)
    b2dma(1)

    hps_free = {0: [("ac", acB)], 1: [("ac", acB)]}
    hbuf_free = [[("dv", dvB)], [("dv", dvB)]]
    ypair_free = {0: [("dv", dvB), ("so", s_acc0)], 1: [("dv", dvB), ("so", s_acc0)]}
    last_pe_of_par = {0: 0, 1: 0}

    for n, (e_i, fb) in enumerate(blocks):
        par = n % 2
        gwait = g_blk[n]
        for tg in range(8):
            hpar = tg % 2
            a_g_list = {}

            def build_y(fc):
                for sub in range(2):
                    ypair = pair01 if sub == 0 else pair23
                    stop_here = (fc == 7 and fb != NFB - 1)

                    def mm_y(e, par=par, fc=fc, sub=sub, ypair=ypair, hpar=hpar,
                             first=(fc == 0), stop_here=stop_here):
                        e.matmul(ypair[:, 0:512],
                                 h_b[hpar][:, fc * 256 + sub * 128: fc * 256 + sub * 128 + 128],
                                 w2_b[par][:, fc * 1024: fc * 1024 + 512],
                                 start=first, stop=stop_here)
                        return e.matmul(ypair[:, 512:1024],
                                        h_b[hpar][:, fc * 256 + sub * 128: fc * 256 + sub * 128 + 128],
                                        w2_b[par][:, fc * 1024 + 512: fc * 1024 + 1024],
                                        start=first, stop=stop_here)
                    w = [("ac", a_g_list[fc])]
                    if fc == 0:
                        w = w + ypair_free[sub]
                    p.op("tensor", mm_y, waits=w, sig="pe", inc=1)

            for fc in range(8):
                hps, hpi = (s4, 0) if fc % 2 == 0 else (s5, 1)

                def mm_h(e, par=par, fc=fc, tg=tg, hps=hps):
                    ins = None
                    for cc in range(8):
                        ins = e.matmul(
                            hps[:, 0:256],
                            w1_b[par][:, cc * 1024 + fc * 128: cc * 1024 + (fc + 1) * 128],
                            xn_own[:, cc * TQ + tg * 256: cc * TQ + (tg + 1) * 256],
                            start=(cc == 0), stop=(cc == 7))
                    return ins
                pe_h = p.op("tensor", mm_h, waits=[("gx", gwait)] + hps_free[hpi],
                            sig="pe", inc=1)
                bcol = e_i * 32 + fb * 8 + fc
                a_g = p.op("scalar", lambda e, hps=hps, hpar=hpar, fc=fc, bcol=bcol: e.activation(
                    out=h_b[hpar][:, fc * 256:(fc + 1) * 256],
                    in_=hps[:, 0:256], func=AF.Gelu,
                    bias=b1r_sb[:, bcol:bcol + 1]),
                    waits=[("pe", pe_h)] + (hbuf_free[hpar] if fc == 0 else []),
                    sig="ac", inc=1)
                hps_free[hpi] = [("ac", a_g)]
                a_g_list[fc] = a_g
                if fc > 0:
                    build_y(fc - 1)
            build_y(7)
            hbuf_free[hpar] = [("pe", p.cur("pe"))]
            for sub in range(2):
                ypair = pair01 if sub == 0 else pair23
                if fb == NFB - 1:
                    bpar = e_i % 2
                    pe_y = p.op("tensor", lambda e, bpar=bpar, ypair=ypair: (
                        e.matmul(ypair[:, 0:512], onesr_bf[:, :],
                                 b2e_bf[bpar][0:1, 0:512],
                                 start=False, stop=True),
                        e.matmul(ypair[:, 512:1024], onesr_bf[:, :],
                                 b2e_bf[bpar][0:1, 512:C],
                                 start=False, stop=True))[-1],
                        waits=[("sa", b2_loaded[e_i])], sig="pe", inc=1)
                else:
                    pe_y = p.cur("pe")
                sg = tg * 2 + sub

                def stt(e, e_i=e_i, sg=sg, ypair=ypair):
                    e.scalar_tensor_tensor(
                        out=acc[:, sg * C: sg * C + 512],
                        in0=ypair[:, 0:512],
                        scalar=msc_sb[:, sg * E + e_i: sg * E + e_i + 1],
                        in1=acc[:, sg * C: sg * C + 512],
                        op0=AluOpType.mult, op1=AluOpType.add)
                    return e.scalar_tensor_tensor(
                        out=acc[:, sg * C + 512:(sg + 1) * C],
                        in0=ypair[:, 512:1024],
                        scalar=msc_sb[:, sg * E + e_i: sg * E + e_i + 1],
                        in1=acc[:, sg * C + 512:(sg + 1) * C],
                        op0=AluOpType.mult, op1=AluOpType.add)
                d_s = p.op("vector", stt, waits=[("pe", pe_y)], sig="dv", inc=1)
                ypair_free[sub] = [("dv", d_s)]
                if DBG and e_i == 0 and fb == 0 and tg == 0 and sub == 1:
                    gdum = p.op("gpsimd", lambda e: e.dma_start(out=d_acc0, in_=acc[:, :]),
                                waits=[("dv", d_s)], sig="gd", inc=16)
                    ypair_free[0] = ypair_free[0] + [("gd", gdum)]
                    ypair_free[1] = ypair_free[1] + [("gd", gdum)]
        if DBG and e_i == 0 and fb == 0 and tg == 7 and False:
            pass
        last_pe_of_par[par] = p.cur("pe")

        if fb == NFB - 1:
            b2_free[e_i % 2] = [("pe", p.cur("pe"))]
            if e_i + 2 < E:
                b2dma(e_i + 2)
        if n + 2 < len(blocks):
            g_blk[n + 2] = wdma(n + 2, [("pe", last_pe_of_par[par])])

    pe_endC = p.cur("pe")

    g1 = p.op("gpsimd", lambda e: e.dma_start(
        out=ab1[:, :].rearrange("p (a c) -> p a c", c=C),
        in_=attn_own[0:1024, :].rearrange("(a p) c -> p a c", p=128)),
        waits=[("cc", 1), ("pe", pe_endC)], sig="gd", inc=16)
    g2 = p.op("gpsimd", lambda e: e.dma_start(
        out=ab2[:, :].rearrange("p (a c) -> p a c", c=C),
        in_=attn_own[1024:2048, :].rearrange("(a p) c -> p a c", p=128)),
        sig="gd", inc=16)
    d_pre = p.cur("dv")
    for sg in range(16):
        src = ab1 if sg < 8 else ab2
        so = (sg % 8) * C
        p.op("vector", lambda e, sg=sg, src=src, so=so: e.tensor_add(
            acc[:, sg * C:(sg + 1) * C], acc[:, sg * C:(sg + 1) * C],
            src[:, so:so + C]),
            waits=[("gd", g2), ("dv", d_pre)], sig="dv", inc=1)
    d_fin = p.cur("dv")
    p.op("sync", lambda e: e.dma_start(
        out=out.rearrange("(a p) c -> p a c", p=128),
        in_=acc[:, :].rearrange("p (a c) -> p a c", c=C)),
        waits=[("dv", d_fin)], sig="so", inc=16)
    p.op("sync", lambda e: e.nop(), waits=[("so", p.cur("so"))])

    with nc.Block() as block:
        @block.sync
        def _(e):
            p.emit("sync", e)

        @block.gpsimd
        def _(e):
            p.emit("gpsimd", e)

        @block.tensor
        def _(e):
            p.emit("tensor", e)

        @block.vector
        def _(e):
            p.emit("vector", e)

        @block.scalar
        def _(e):
            p.emit("scalar", e)

    return nc



def _rope_perm():
    return np.array([2 * i for i in range(32)] + [2 * i + 1 for i in range(32)])


def _constants():
    f32 = np.float32
    pos = np.arange(T, dtype=f32)
    inv_freq = (1.0 / np.float32(10000.0) ** (np.arange(0, HD, 2, dtype=f32) / np.float32(HD))).astype(f32)
    freqs = pos[:, None] * inv_freq[None, :]
    cosT = np.cos(freqs).astype(f32)
    sinT = np.sin(freqs).astype(f32)
    cosf = np.empty((128, T), dtype=f32)
    sinf = np.empty((128, T), dtype=f32)
    for pp in range(128):
        cosf[pp] = cosT[:, pp % 32]
        sinf[pp] = sinT[:, pp % 32]
    rotm = np.zeros((128, 128), dtype=f32)
    for b0 in (0, 64):
        for i in range(32):
            rotm[b0 + i + 32, b0 + i] = -1.0
            rotm[b0 + i, b0 + i + 32] = 1.0
    sel128 = np.zeros((128, 128), dtype=f32)
    sel128[0, 0:64] = 1.0
    sel128[64, 64:128] = 1.0
    sel2 = np.zeros((2, 128), dtype=f32)
    sel2[0, 0:64] = 1.0
    sel2[1, 64:128] = 1.0
    hmask2 = np.zeros((128, 2), dtype=f32)
    hmask2[0:64, 0] = 1.0
    hmask2[64:128, 1] = 1.0
    identf = np.eye(128, dtype=f32)
    s_i = np.arange(128)[:, None]
    u_i = np.arange(896)[None, :]
    maskP = (u_i >= s_i + 384).astype(f32)
    onesc = np.ones((128, 2), dtype=f32)
    rowsel = np.zeros((2, 256), dtype=f32)
    rowsel[0, 0:128] = 1.0
    rowsel[1, 128:256] = 1.0
    onesr = np.ones((1, 128), dtype=f32)
    return dict(cosf=cosf, sinf=sinf, rotm=rotm, sel128=sel128, sel2=sel2,
                hmask2=hmask2, identf=identf, maskP=maskP, onesc=onesc,
                onesr=onesr, rowsel=rowsel)


def kernel(L0_attn_x, L0_attn_w_attn, L0_attn_w_proj, L0_ffn_w_red, L0_ffn_wg,
           L0_ffn_w1, L0_ffn_b1, L0_ffn_w2, L0_ffn_b2):
    f32 = np.float32
    x = np.asarray(L0_attn_x, dtype=f32)
    w_attn = np.asarray(L0_attn_w_attn, dtype=f32)
    w_proj = np.asarray(L0_attn_w_proj, dtype=f32)
    w_red = np.ascontiguousarray(np.asarray(L0_ffn_w_red, dtype=f32))
    wg = np.ascontiguousarray(np.asarray(L0_ffn_wg, dtype=f32))
    w1 = np.ascontiguousarray(np.asarray(L0_ffn_w1, dtype=f32))
    b1 = np.asarray(L0_ffn_b1, dtype=f32)
    w2 = np.ascontiguousarray(np.asarray(L0_ffn_w2, dtype=f32))
    b2 = np.asarray(L0_ffn_b2, dtype=f32)

    perm = _rope_perm()
    consts = _constants()
    b1r = np.ascontiguousarray(b1.reshape(E, 32, 128).transpose(2, 0, 1).reshape(128, E * 32))
    b2c = np.ascontiguousarray(b2.reshape(1, E * C))
    wgT = np.ascontiguousarray(wg.T)

    in_maps = []
    for core in range(8):
        b = core // 4
        g = core % 4
        xb = x[b]
        xT_a = np.ascontiguousarray(xb.T)
        xo_a = np.ascontiguousarray(xb[g * TQ:(g + 1) * TQ])
        xoT_a = np.ascontiguousarray(xo_a.T)
        qcols = []
        for h in range(4 * g, 4 * g + 4):
            qcols.extend((h * HD + perm).tolist())
        wq_a = np.ascontiguousarray(w_attn[:, qcols])
        kcols = (C + g * HD + perm).tolist()
        wkd_a = np.ascontiguousarray(w_attn[:, kcols + kcols])
        vbase = C + NKV * HD + g * HD
        wv_a = np.ascontiguousarray(w_attn[:, vbase:vbase + HD])
        wproj_a = np.ascontiguousarray(w_proj[4 * g * HD:(4 * g + 4) * HD, :])
        m = dict(xT=xT_a, xtm=xb, xoT=xoT_a, xo=xo_a, wq=wq_a, wkd=wkd_a,
                 wv=wv_a, wproj=wproj_a, wred=w_red, wg=wg, wgT=wgT, w1=w1,
                 b1r=b1r, w2=w2, b2c=b2c, **consts)
        in_maps.append(m)

    nc = build_program()
    res = run_bass_kernel_spmd(nc, in_maps, core_ids=list(range(8)))
    if DBG:
        global DBG_RES
        DBG_RES = res.results
    outp = np.empty((B, T, C), dtype=f32)
    for core in range(8):
        b = core // 4
        g = core % 4
        outp[b, g * TQ:(g + 1) * TQ] = res.results[core]["out"]
    return outp
